# version 08
# baseline (speedup 1.0000x reference)
import os
import numpy as np
import ml_dtypes
import concourse.bass as bass
import concourse.mybir as mybir
from concourse.bass_utils import run_bass_kernel_spmd

F32 = mybir.dt.float32
BF16 = mybir.dt.bfloat16
AF = mybir.ActivationFunctionType
OP = mybir.AluOpType
AX = mybir.AxisListType

B, D = 4, 2048
NH, HD, NHL = 8, 128, 4
SCALE = float(HD ** -0.5)
LAMBDA_INIT = 0.8 - 0.6 * float(np.exp(-0.3 * 12))
EPS_P = 1e-5
EPS_A = 1e-6
VD, VSTR = 256, 257
HF, HFS = 8192, 1024
KSTEPS = 4
NCHK = D // 128

WO_PERM = []
for _h in range(4):
    WO_PERM += [2 * _h, 2 * _h + 1, 8 + 2 * _h, 9 + 2 * _h]

PAIRS = [[0, 1], [2, 3], [4, 5], [6, 7]]
ALL8 = [list(range(8))]
ENGS = ("sync", "gpsimd", "scalar", "vector", "tensor")


class _Inst:
    __slots__ = ("w", "inst")

    def __init__(self, w, inst):
        self.w = w
        self.inst = inst

    def inc(self, sem, n=1):
        self.w.led[sem] = self.w.led.get(sem, 0) + n
        if self.inst is not None:
            self.inst.then_inc(self.w.sems[sem], n)
        return self


class _Eng:
    __slots__ = ("_w", "_name")

    def __init__(self, w, name):
        self._w = w
        self._name = name

    def wait(self, sem, val):
        if self._w.active == self._name and val > 0:
            self._w.eng.wait_ge(self._w.sems[sem], val)

    def drain(self):
        if self._w.active == self._name:
            self._w.eng.drain()

    def __getattr__(self, attr):
        w = self._w
        nm = self._name
        if w.active == nm:
            real = getattr(w.eng, attr)

            def call(*a, **k):
                ii = _Inst(w, real(*a, **k))
                w.last[nm] = ii
                w.last_kind[nm] = attr
                return ii
        else:
            def call(*a, **k):
                w.last_kind[nm] = attr
                return _Inst(w, None)
        return call


class Walk:
    def __init__(self, sems, active, eng, seed):
        self.sems = sems
        self.active = active
        self.eng = eng
        self.led = dict(seed)
        self.last = {}
        self.last_kind = {}
        self.T = _Eng(self, "tensor")
        self.V = _Eng(self, "vector")
        self.S = _Eng(self, "scalar")
        self.Y = _Eng(self, "sync")
        self.G = _Eng(self, "gpsimd")

    def c(self, sem):
        return self.led.get(sem, 0)

    def all_wait(self, sem, val):
        for e in (self.T, self.V, self.S, self.Y, self.G):
            e.wait(sem, val)

    def fence(self):
        for nm in ("tensor", "vector", "scalar"):
            if self.active == nm:
                self.eng.drain().then_inc(self.sems["s_fence"], 1)
        self.led["s_fence"] = self.led.get("s_fence", 0) + 3

    def block_start(self, fen, dma_sems):
        self.all_wait("s_fence", fen)
        for s, v in dma_sems:
            self.all_wait(s, v)

    def dma(self, E, sem, *a, **kw):
        E.wait(sem, self.c(sem))
        return E.dma_start(*a, **kw).inc(sem, 16)


def _stage(inp, SS):
    x = np.ascontiguousarray(np.asarray(inp["L0_attn_x"], dtype=np.float32))
    wq = np.asarray(inp["L0_attn_Wq"], dtype=np.float32)
    wk = np.asarray(inp["L0_attn_Wk"], dtype=np.float32)
    wv = np.asarray(inp["L0_attn_Wv"], dtype=np.float32)
    wo = np.asarray(inp["L0_attn_Wo"], dtype=np.float32)
    wo_perm = np.concatenate([wo[blk * 128:(blk + 1) * 128] for blk in WO_PERM], axis=0)
    subln = np.asarray(inp["L0_attn_subln_weight"], dtype=np.float32)
    zeros256 = np.zeros(256, dtype=np.float32)
    in_maps = []
    for c in range(8):
        b, g = c // 2, c % 2
        cols = slice(4 * g * 256, (4 * g + 4) * 256)
        m = {
            "x_full": x[b],
            "x_half": x[b, g * (SS // 2):(g + 1) * (SS // 2)],
            "x_last": x[b, SS - 1: SS],
            "wq": wq[:, cols], "wk": wk[:, cols], "wv": wv[:, cols],
            "wo_perm": wo_perm,
            "gate_w": np.asarray(inp["L0_ffn_gate_w"], dtype=np.float32),
            "agg_w": np.asarray(inp["L0_ffn_agg_w"], dtype=np.float32),
            "bc_w": np.asarray(inp["L0_ffn_bc_w"], dtype=np.float32),
            "d1s": np.asarray(inp["L0_ffn_d1_w"], dtype=np.float32)[:, c * HFS:(c + 1) * HFS],
            "d2s": np.asarray(inp["L0_ffn_d2_w"], dtype=np.float32)[c * HFS:(c + 1) * HFS, :],
            "d1bs": np.asarray(inp["L0_ffn_d1_b"], dtype=np.float32)[None, c * HFS:(c + 1) * HFS],
            "lamq1": np.asarray(inp["L0_attn_lambda_q1"], dtype=np.float32)[None, :],
            "lamk1": np.asarray(inp["L0_attn_lambda_k1"], dtype=np.float32)[None, :],
            "lamq2": np.asarray(inp["L0_attn_lambda_q2"], dtype=np.float32)[None, :],
            "lamk2": np.asarray(inp["L0_attn_lambda_k2"], dtype=np.float32)[None, :],
            "subl_a": (subln if g == 0 else zeros256)[None, :],
            "subl_b": (subln if g == 1 else zeros256)[None, :],
            "in_gamma": np.asarray(inp["L0_ffn_in_gamma"], dtype=np.float32)[None, :],
            "in_beta": np.asarray(inp["L0_ffn_in_beta"], dtype=np.float32)[None, :],
            "pn_gamma": np.asarray(inp["L0_ffn_pn_gamma"], dtype=np.float32)[None, :],
            "pn_beta": np.asarray(inp["L0_ffn_pn_beta"], dtype=np.float32)[None, :],
            "out_gamma": np.asarray(inp["L0_ffn_out_gamma"], dtype=np.float32)[None, :],
            "out_beta": np.asarray(inp["L0_ffn_out_beta"], dtype=np.float32)[None, :],
            "agg_b": np.asarray(inp["L0_ffn_agg_b"], dtype=np.float32)[None, :],
            "bc_b": np.asarray(inp["L0_ffn_bc_b"], dtype=np.float32)[None, :],
            "d2_b": np.asarray(inp["L0_ffn_d2_b"], dtype=np.float32)[None, :],
            "gate_b": np.asarray(inp["L0_ffn_gate_b"], dtype=np.float32)[None, :],
            "sel_b": np.eye(4, dtype=np.float32)[:, b:b + 1],
        }
        in_maps.append({k: np.ascontiguousarray(v) for k, v in m.items()})
    return in_maps


SEM_NAMES = ["ini", "s_x", "s_w", "s_stat", "s_aux", "s_ln", "s_tp", "s_ev", "s_mm",
             "s_sc", "s_ex", "s_mk", "s_av", "s_cmb", "s_ast", "s_cc", "s_th",
             "s_qld", "s_fence", "s_bld", "s_bmm", "s_bev", "s_out", "s_row",
             "g_ld", "g_st", "s_x2", "s_q2", "s_q3", "s_q4", "s_st2", "s_a2",
             "s_t1", "s_t2"]


def build_nc(SS):
    DBG = os.environ.get("KDBG") == "1"
    DK = "ExternalOutput" if DBG else "Internal"
    NT = SS // 128
    NTH = NT // 2
    NQG = SS // 512
    nc = bass.Bass("TRN2", target_bir_lowering=False, dynamic_dma_scratch_size=8192)

    dr = {}

    def din(name, shape):
        dr[name] = nc.dram_tensor(name, list(shape), F32, kind="ExternalInput").ap()

    din("x_full", (SS, D)); din("x_half", (SS // 2, D)); din("x_last", (1, D))
    din("wq", (D, 1024)); din("wk", (D, 1024)); din("wv", (D, 1024))
    din("wo_perm", (D, D)); din("gate_w", (D, D)); din("agg_w", (D, D)); din("bc_w", (D, D))
    din("d1s", (D, HFS)); din("d2s", (HFS, D)); din("d1bs", (1, HFS))
    for n in ("lamq1", "lamk1", "lamq2", "lamk2"):
        din(n, (1, 128))
    din("subl_a", (1, 256)); din("subl_b", (1, 256))
    for n in ("in_gamma", "in_beta", "pn_gamma", "pn_beta", "out_gamma", "out_beta",
              "agg_b", "bc_b", "d2_b", "gate_b"):
        din(n, (1, D))
    din("sel_b", (4, 1))
    out_d = nc.dram_tensor("out", [SS // 2, D], F32, kind="ExternalOutput").ap()

    q_scr = nc.dram_tensor("q_scr", [NHL, 2, 128, SS], BF16, kind=DK)
    k_scr = nc.dram_tensor("k_scr", [NHL, 2, 128, SS], BF16, kind="Internal")
    v_scr = nc.dram_tensor("v_scr", [NHL, NT, 128, VD], BF16, kind="Internal")
    contrib = [nc.dram_tensor(f"contrib{h}", [2, SS // 2, 512], BF16, kind="Internal")
               for h in range(NHL)]
    rs_out = [nc.dram_tensor(f"rs_out{h}", [SS // 2, 512], BF16, kind="Internal")
              for h in range(NHL)]
    a31_i = nc.dram_tensor("a31_i", [1, 1024], BF16, kind="Internal")
    a31_o = nc.dram_tensor("a31_o", [2, 1024], BF16, kind="Internal")
    a31_j = nc.dram_tensor("a31_j", [1, 1024], BF16, kind="Internal")
    a31_p = nc.dram_tensor("a31_p", [2, 1024], BF16, kind="Internal")
    z4_i = nc.dram_tensor("z4_i", [1, D], F32, kind="Internal")
    z4_o = nc.dram_tensor("z4_o", [8, D], F32, kind="Internal")
    ar_i = [nc.dram_tensor(f"ar_i{k}", [4, D], F32, kind="Internal") for k in range(KSTEPS)]
    ar_o = [nc.dram_tensor(f"ar_o{k}", [4, D], F32, kind="Internal") for k in range(KSTEPS)]
    bc_scr = nc.dram_tensor("bc_scr", [1, D + 8], F32, kind=DK)
    wo_bf = nc.dram_tensor("wo_bf", [D, D], BF16, kind="Internal")
    gw_bf = nc.dram_tensor("gw_bf", [D, D], BF16, kind="Internal")
    agg_bf = nc.dram_tensor("agg_bf", [D, D], BF16, kind="Internal")
    bcw_bf = nc.dram_tensor("bcw_bf", [D, D], BF16, kind="Internal")
    d1_bf = nc.dram_tensor("d1_bf", [D, HFS], BF16, kind="Internal")
    d2_bf = nc.dram_tensor("d2_bf", [HFS, D], BF16, kind="Internal")
    rb_bf = nc.dram_tensor("rb_bf", [4, D], BF16, kind="Internal")
    if DBG:
        dbg_b = nc.dram_tensor("dbg_b", [8, 128, D], F32, kind="ExternalOutput")
        dbg_t = nc.dram_tensor("dbg_t", [3, 128, D], BF16, kind="ExternalOutput")
        dbg_id = nc.dram_tensor("dbg_id", [14, 128, 128], BF16, kind="ExternalOutput")
        dbg_ct = nc.dram_tensor("dbg_ct", [2, SS // 2, 512], BF16, kind="ExternalOutput")
        dbg_rs = nc.dram_tensor("dbg_rs", [SS // 2, 512], BF16, kind="ExternalOutput")
        dbg_z4i = nc.dram_tensor("dbg_z4i", [1, D], F32, kind="ExternalOutput")
        dbg_z4o = nc.dram_tensor("dbg_z4o", [8, D], F32, kind="ExternalOutput")

    ctx_stack = []

    def sbuf(name, shape, dt):
        h = nc.sbuf_tensor(name, list(shape), dt)
        t = h.__enter__()
        ctx_stack.append(h)
        return t

    def close_last(n):
        for _ in range(n):
            ctx_stack.pop().__exit__(None, None, None)

    nc.allow_low_precision("bf16 matmul operands, fp32 accumulation").__enter__()
    ps = nc.psum_tensor("ps", [128, 4096], F32).__enter__()
    sems = {n: nc.semaphore(n).__enter__() for n in SEM_NAMES}

    identf = sbuf("identf", [128, 128], F32)
    identb = sbuf("identb", [128, 128], BF16)
    tri = sbuf("tri", [128, 128], BF16)
    subl_at = sbuf("subl_at", [128, 256], F32)
    subl_bt = sbuf("subl_bt", [128, 256], F32)
    neglam = sbuf("neglam", [128, 1], F32)
    ones1b = sbuf("ones1b", [1, 128], BF16)
    colsc = sbuf("colsc", [128, 32], F32)
    iop = sbuf("iop", [128, 1], F32)
    iof = sbuf("iof", [128, 128], F32)
    lrow = sbuf("lrow", [1, 512], F32)
    lsc = sbuf("lsc", [1, 8], F32)

    seed = [{}]

    def run_block(emitfn):
        w0 = Walk(sems, None, None, seed[0])
        emitfn(w0)
        with nc.Block() as block:
            for name in ENGS:
                getattr(block, name)(lambda e, n=name: emitfn(Walk(sems, n, e, seed[0])))
        seed[0] = w0.led

    def ln_scalars(w, base, n_el, eps, pre_wait=None):
        V, S_ = w.V, w.S
        cs = lambda i: colsc[:, base + i: base + i + 1]
        if pre_wait is not None:
            V.wait(*pre_wait)
        V.drain()
        V.tensor_scalar(cs(4), cs(0), 1.0 / n_el, None, OP.mult)
        V.tensor_scalar(cs(5), cs(1), 1.0 / n_el, None, OP.mult)
        V.drain()
        V.tensor_scalar(cs(1), cs(4), cs(4), None, OP.mult)
        V.drain()
        V.tensor_tensor(cs(5), cs(5), cs(1), OP.subtract)
        V.drain()
        V.tensor_scalar(cs(5), cs(5), float(eps), None, OP.add).inc("s_stat")
        S_.wait("s_stat", w.c("s_stat"))
        S_.activation(cs(5), cs(5), AF.Sqrt).inc("s_aux")
        V.wait("s_aux", w.c("s_aux"))
        V.reciprocal(cs(2), cs(5))
        V.drain()
        V.tensor_scalar(cs(3), cs(4), cs(2), -1.0, OP.mult, OP.mult).inc("s_stat")
        V.drain()
        return w.c("s_stat")

    def emit_prolog(w):
        T, V, S_, Y, G = w.T, w.V, w.S, w.Y, w.G
        G.iota(iop[:, :], pattern=[[0, 1]], channel_multiplier=1,
               allow_small_or_imprecise_dtypes=True)
        G.iota(iof[:, :], pattern=[[1, 128]], channel_multiplier=0,
               allow_small_or_imprecise_dtypes=True).inc("ini")
        for i, n in enumerate(("lamq1", "lamk1", "lamq2", "lamk2")):
            w.dma(G, "g_ld", out=lrow[0:1, i * 128:(i + 1) * 128], in_=dr[n])
        w.dma(Y, "s_qld", out=subl_at[:, :], in_=dr["subl_a"].broadcast_to([128, 256]))
        w.dma(Y, "s_qld", out=subl_bt[:, :], in_=dr["subl_b"].broadcast_to([128, 256]))
        ldv = w.c("g_ld")
        ldy = w.c("s_qld")
        for srcn, dst in (("wo_perm", wo_bf), ("gate_w", gw_bf), ("agg_w", agg_bf),
                          ("bc_w", bcw_bf), ("d1s", d1_bf), ("d2s", d2_bf)):
            w.dma(G, "s_w", out=dst.ap(), in_=dr[srcn])
        w.dma(G, "s_w", out=rb_bf.ap()[0:1, :], in_=dr["agg_b"])
        w.dma(G, "s_w", out=rb_bf.ap()[1:2, :], in_=dr["bc_b"])
        w.dma(G, "s_w", out=rb_bf.ap()[2:3, :], in_=dr["gate_b"])
        w.dma(G, "s_w", out=rb_bf.ap()[3:4, 0:HFS], in_=dr["d1bs"])
        V.wait("ini", 1)
        V.wait("g_ld", ldv)
        V.wait("s_qld", ldy)
        V.tensor_scalar(tri[:, :], iof[:, :], iop[:, :], None, OP.is_ge)
        V.tensor_scalar(identf[:, :], iof[:, :], iop[:, :], None, OP.is_equal)
        V.tensor_scalar(identb[:, :], iof[:, :], iop[:, :], None, OP.is_equal)
        V.memset(ones1b[:, :], 1.0)
        V.tensor_scalar(subl_at[:, :], subl_at[:, :], 1.0 - LAMBDA_INIT, None, OP.mult)
        V.tensor_scalar(subl_bt[:, :], subl_bt[:, :], 1.0 - LAMBDA_INIT, None, OP.mult)
        V.tensor_tensor(lrow[0:1, 0:128], lrow[0:1, 0:128], lrow[0:1, 128:256], OP.mult)
        V.tensor_tensor(lrow[0:1, 256:384], lrow[0:1, 256:384], lrow[0:1, 384:512], OP.mult)
        V.drain()
        V.reduce_sum(lsc[0:1, 0:1], lrow[0:1, 0:128], axis=AX.X)
        V.reduce_sum(lsc[0:1, 1:2], lrow[0:1, 256:384], axis=AX.X).inc("s_stat")
        S_.wait("s_stat", w.c("s_stat"))
        S_.activation(lsc[0:1, 2:3], lsc[0:1, 0:1], AF.Exp)
        S_.activation(lsc[0:1, 3:4], lsc[0:1, 1:2], AF.Exp).inc("s_aux")
        V.wait("s_aux", w.c("s_aux"))
        V.tensor_tensor(lsc[0:1, 4:5], lsc[0:1, 3:4], lsc[0:1, 2:3], OP.subtract)
        V.drain()
        V.tensor_scalar(lsc[0:1, 5:6], lsc[0:1, 4:5], -LAMBDA_INIT, None, OP.add).inc("s_stat")
        Y.wait("s_stat", w.c("s_stat"))
        w.dma(Y, "s_ast", out=bc_scr.ap()[0:1, D:D + 1], in_=lsc[0:1, 5:6],
              allow_slow_non_contiguous=True)
        Y.wait("s_ast", w.c("s_ast"))
        w.dma(Y, "s_x", out=neglam[:, :],
              in_=bc_scr.ap()[0:1, D:D + 1].broadcast_to([128, 1]),
              allow_slow_non_contiguous=True)
        w.fence()

    run_block(emit_prolog)
    close_last(4)
    WPRE = seed[0]["s_w"]

    lnT = sbuf("lnT", [128, NCHK * SS], BF16)
    xt = [sbuf(f"xt{i}", [128, D], F32) for i in range(2)]
    sqs = sbuf("sqs", [128, D], F32)
    lnt_t = [sbuf(f"lnt{i}", [128, D], BF16) for i in range(2)]
    FEN0 = seed[0]["s_fence"]
    DSEM0 = [(s, seed[0].get(s, 0)) for s in ["s_x", "s_x2", "s_w", "s_qld", "s_q2", "s_q3", "s_q4", "s_ast", "s_st2", "s_a2", "s_bld", "s_out", "s_cc", "g_ld", "g_st", "s_t1", "s_t2"]]

    def emit_a1a(w):
        T, V, S_, Y, G = w.T, w.V, w.S, w.Y, w.G
        w.block_start(FEN0, DSEM0)
        for t in range(NT):
            bi = t % 2
            sx = "s_x" if bi == 0 else "s_x2"
            sx2 = "s_q3" if bi == 0 else "s_q4"
            if t >= 2:
                Y.wait("s_ln", t - 1)
            w.dma(Y, sx, out=xt[bi][:, 0:1024], in_=dr["x_full"][t * 128:(t + 1) * 128, 0:1024])
            w.dma(Y, sx2, out=xt[bi][:, 1024:2048], in_=dr["x_full"][t * 128:(t + 1) * 128, 1024:2048])
            lv = [(sx, w.c(sx)), (sx2, w.c(sx2))]
            base = 8 * bi
            for s_, v_ in lv:
                V.wait(s_, v_)
            if t >= 2:
                V.wait("s_ln", t - 1)
            V.reduce_sum(colsc[:, base:base + 1], xt[bi][:, :], axis=AX.X)
            for s_, v_ in lv:
                S_.wait(s_, v_)
            S_.activation(sqs[:, :], xt[bi][:, :], AF.Square,
                          accum_out=colsc[:, base + 1:base + 2]).inc("s_aux")
            ln_scalars(w, base, D, EPS_P, pre_wait=("s_aux", w.c("s_aux")))
            if t >= 2:
                V.wait("s_tp", 16 * (t - 1))
            V.tensor_scalar(lnt_t[bi][:, :], xt[bi][:, :],
                            colsc[:, base + 2:base + 3],
                            colsc[:, base + 3:base + 4], OP.mult, OP.add).inc("s_ln")
            lnv = w.c("s_ln")
            for qd in range(4):
                slot = qd % 2
                T.wait("s_ln", lnv)
                if t * 4 + qd >= 2:
                    T.wait("s_ev", w.c("s_ev") - 1)
                for j in range(4):
                    c = qd * 4 + j
                    T.transpose(
                        ps[:, slot * 512 + j * 64: slot * 512 + (j + 1) * 64].bitcast(BF16),
                        lnt_t[bi][:, c * 128:(c + 1) * 128], identb[:, :]).inc("s_tp")
                V.wait("s_tp", w.c("s_tp"))
                for j in range(4):
                    c = qd * 4 + j
                    ii = V.tensor_copy(lnT[:, c * SS + t * 128: c * SS + (t + 1) * 128],
                                       ps[:, slot * 512 + j * 64: slot * 512 + (j + 1) * 64].bitcast(BF16))
                    if j == 3:
                        ii.inc("s_ev")
        w.fence()

    run_block(emit_a1a)
    close_last(5)

    wsb = sbuf("wsb", [128, NCHK * 1024], BF16)
    stg = [sbuf(f"stg{i}", [128, 512], BF16) for i in range(2)]
    FEN1 = seed[0]["s_fence"]
    DSEM1 = [(s, seed[0].get(s, 0)) for s in ["s_x", "s_x2", "s_w", "s_qld", "s_q2", "s_q3", "s_q4", "s_ast", "s_st2", "s_a2", "s_bld", "s_out", "s_cc", "g_ld", "g_st", "s_t1", "s_t2"]]
    EV1 = seed[0]["s_ev"]

    def emit_a1b(w):
        T, V, S_, Y, G = w.T, w.V, w.S, w.Y, w.G
        w.block_start(FEN1, DSEM1)
        nmm = [0]

        def mm_unit(stats, movs, n_out, scr_ap, wv):
            slot = nmm[0] % 2
            ssem = "s_ast" if slot == 0 else "s_st2"
            T.wait("s_w", wv)
            if nmm[0] == 0:
                T.wait("s_ev", EV1)
            if nmm[0] >= 2:
                T.wait("s_mm", w.c("s_mm") - 1)
            for ci in range(NCHK):
                ii = T.matmul(ps[:, 2048 + slot * 512: 2048 + slot * 512 + n_out],
                              stats[ci], movs[ci], start=(ci == 0), stop=(ci == NCHK - 1))
                if ci == NCHK - 1:
                    ii.inc("s_sc")
            V.wait("s_sc", w.c("s_sc"))
            if nmm[0] >= 2:
                V.wait(ssem, w.c(ssem))
            V.tensor_copy(stg[slot][:, 0:n_out],
                          ps[:, 2048 + slot * 512: 2048 + slot * 512 + n_out]).inc("s_mm")
            Y.wait("s_mm", w.c("s_mm"))
            w.dma(Y, ssem, out=scr_ap, in_=stg[slot][:, 0:n_out])
            nmm[0] += 1

        for wname, scr in (("wq", q_scr), ("wk", k_scr)):
            G.wait("s_mm", w.c("s_mm"))
            w.dma(G, "s_w", out=wsb[:, :].rearrange("p (c n) -> p c n", c=NCHK),
                  in_=dr[wname].rearrange("(c p) n -> p c n", p=128))
            wv = w.c("s_w")
            for h in range(NHL):
                for br in range(2):
                    off = h * 256 + br * 128
                    for tg in range(SS // 512):
                        mm_unit(
                            [wsb[:, c * 1024 + off: c * 1024 + off + 128] for c in range(NCHK)],
                            [lnT[:, c * SS + tg * 512: c * SS + tg * 512 + 512] for c in range(NCHK)],
                            512, scr.ap()[h, br, :, tg * 512:(tg + 1) * 512], wv)
        G.wait("s_mm", w.c("s_mm"))
        w.dma(G, "s_w", out=wsb[:, :].rearrange("p (c n) -> p c n", c=NCHK),
              in_=dr["wv"].rearrange("(c p) n -> p c n", p=128))
        wv = w.c("s_w")
        for h in range(NHL):
            for t in range(NT):
                mm_unit(
                    [lnT[:, c * SS + t * 128: c * SS + (t + 1) * 128] for c in range(NCHK)],
                    [wsb[:, c * 1024 + h * 256: c * 1024 + h * 256 + VD] for c in range(NCHK)],
                    VD, v_scr.ap()[h, t, :, :], wv)
        w.fence()

    run_block(emit_a1b)
    close_last(4)
    AST_QKV = [("s_ast", seed[0].get("s_ast", 0)), ("s_st2", seed[0].get("s_st2", 0))]

    ksb = sbuf("ksb", [128, 2 * SS], BF16)
    qsb = sbuf("qsb", [128, 2 * SS], BF16)
    vstrip = sbuf("vstrip", [128, NT * VSTR], BF16)
    pts = [sbuf(f"pts{br}", [128, NT * 512], BF16) for br in range(2)]
    osb = sbuf("osb", [128, 3 * VSTR], F32)
    astg = [sbuf(f"astg{i}", [128, 256], BF16) for i in range(2)]
    thw = [sbuf(f"thw{i}", [128, 2048], BF16) for i in range(2)]
    throw = sbuf("throw", [4, D], F32)
    trow = sbuf("trow", [4, D], F32)
    rsq = sbuf("rsq", [4, D], F32)
    trow_b = sbuf("trow_b", [4, D], BF16)
    tT = sbuf("tT", [128, 64], BF16)
    selb = sbuf("selb", [4, 1], F32)
    r4 = sbuf("r4", [4, 16], F32)
    g4a = sbuf("g4a", [4, D], F32)
    g4b = sbuf("g4b", [4, D], F32)
    d2bt = sbuf("d2bt", [4, D], F32)
    zrow = sbuf("zrow", [1, D], F32)
    xlr = sbuf("xlr", [1, D], F32)
    brow = sbuf("brow", [1, 2048], BF16)
    a4c = sbuf("a4c", [128, 16], BF16)
    a4d = sbuf("a4d", [128, 16], BF16)
    FEN2 = seed[0]["s_fence"]
    DSEM2 = [(s, seed[0].get(s, 0)) for s in ["s_x", "s_x2", "s_w", "s_qld", "s_q2", "s_q3", "s_q4", "s_ast", "s_st2", "s_a2", "s_bld", "s_out", "s_cc", "g_ld", "g_st", "s_t1", "s_t2"]]

    def emit_a2(w):
        T, V, S_, Y, G = w.T, w.V, w.S, w.Y, w.G
        w.block_start(FEN2, DSEM2)

        def barrier():
            w.fence()
            fv = w.c("s_fence")
            w.all_wait("s_fence", fv)
            for s in ("s_x", "s_x2", "s_w", "s_qld", "s_q2", "s_q3", "s_q4", "s_ast",
                      "s_st2", "s_a2", "s_bld", "s_out", "s_cc", "g_ld", "g_st",
                      "s_t1", "s_t2"):
                w.all_wait(s, w.c(s))
        if DBG:
            w.dma(Y, "s_q4", out=dbg_id.ap()[2], in_=identb[:, :])
        mem = {"ast_astg": 0, "rowuse": 0}

        ROT = ("s_qld", "s_q2", "s_q3", "s_q4")

        def head_loads(h):
            Y.wait("s_ex", w.c("s_ex"))
            for s, v in AST_QKV:
                Y.wait(s, v)
            i = 0
            for br in range(2):
                w.dma(Y, ROT[i % 4], out=ksb[:, br * SS:(br + 1) * SS],
                      in_=k_scr.ap()[h, br, :, :]); i += 1
                w.dma(Y, ROT[i % 4], out=qsb[:, br * SS:(br + 1) * SS],
                      in_=q_scr.ap()[h, br, :, :]); i += 1
            Y.wait("s_av", w.c("s_av"))
            for t in range(NT):
                w.dma(Y, ROT[i % 4], out=vstrip[:, t * VSTR: t * VSTR + VD],
                      in_=v_scr.ap()[h, t, :, :]); i += 1
            return [(s, w.c(s)) for s in ROT]

        def qgroup(h, qg, ldv, last31=False):
            nkc = 4 * qg + 4
            nquad = (nkc + 3) // 4
            avv_prev = w.c("s_av")
            npair = (nkc + 1) // 2
            for br in range(2):
                for pair in range(npair):
                    slot = pair % 2
                    kcs = [pair * 2 + i for i in range(2) if pair * 2 + i < nkc]
                    if pair == 0:
                        for s, v in ldv:
                            T.wait(s, v)
                    T.wait("s_ex", w.c("s_ex") - 1 if pair >= 2 else w.c("s_ex"))
                    for i, kc in enumerate(kcs):
                        ii = T.matmul(ps[:, slot * 1024 + i * 512: slot * 1024 + i * 512 + 512],
                                      ksb[:, br * SS + kc * 128: br * SS + (kc + 1) * 128],
                                      qsb[:, br * SS + qg * 512: br * SS + qg * 512 + 512],
                                      start=True, stop=True)
                        if i == len(kcs) - 1:
                            ii.inc("s_sc")
                    S_.wait("s_sc", w.c("s_sc"))
                    S_.wait("s_av", avv_prev)
                    wid = len(kcs) * 512
                    S_.activation(pts[br][:, kcs[0] * 512: kcs[0] * 512 + wid],
                                  ps[:, slot * 1024: slot * 1024 + wid],
                                  AF.Exp, scale=SCALE).inc("s_ex")
                exv = w.c("s_ex")
                first = True
                last = None
                for t in range(4):
                    qt = 4 * qg + t
                    if first:
                        V.wait("s_ex", exv)
                        V.wait("s_av", avv_prev)
                        first = False
                    last = V.tensor_tensor(
                        pts[br][:, qt * 512 + t * 128: qt * 512 + (t + 1) * 128],
                        pts[br][:, qt * 512 + t * 128: qt * 512 + (t + 1) * 128],
                        tri[:, :], OP.mult)
                    for kc in range(qt + 1, nkc):
                        last = V.memset(pts[br][:, kc * 512 + t * 128: kc * 512 + (t + 1) * 128], 0.0)
                last.inc("s_mk")
            mkv = w.c("s_mk")
            for t in range(4):
                qt = 4 * qg + t
                for br in range(2):
                    obase = 2048 + br * 512
                    T.wait("s_mk", mkv)
                    T.wait("s_cmb", w.c("s_cmb"))
                    for kc in range(qt + 1):
                        ii = T.matmul(ps[:, obase: obase + VSTR],
                                      pts[br][:, kc * 512 + t * 128: kc * 512 + (t + 1) * 128],
                                      vstrip[:, kc * VSTR: kc * VSTR + VSTR],
                                      start=(kc == 0), stop=(kc == qt))
                        if kc == qt:
                            ii.inc("s_av")
                V.wait("s_av", w.c("s_av"))
                V.wait("s_aux", w.c("s_aux"))
                V.tensor_copy(osb[:, 0:VSTR], ps[:, 2048: 2048 + VSTR])
                V.tensor_copy(osb[:, VSTR: 2 * VSTR], ps[:, 2560: 2560 + VSTR])
                V.drain()
                V.reciprocal(colsc[:, 16:17], osb[:, VD:VD + 1])
                V.reciprocal(colsc[:, 17:18], osb[:, VSTR + VD:VSTR + VD + 1])
                V.drain()
                V.tensor_scalar(colsc[:, 18:19], colsc[:, 17:18], neglam[:, :], None, OP.mult)
                V.tensor_scalar(osb[:, 0:VD], osb[:, 0:VD], colsc[:, 16:17], None, OP.mult)
                V.drain()
                V.scalar_tensor_tensor(osb[:, VSTR:VSTR + VD], osb[:, VSTR:VSTR + VD],
                                       colsc[:, 18:19], osb[:, 0:VD], OP.mult, OP.add).inc("s_cmb")
                S_.wait("s_cmb", w.c("s_cmb"))
                S_.activation(osb[:, 2 * VSTR: 2 * VSTR + VD], osb[:, VSTR:VSTR + VD], AF.Square,
                              accum_out=colsc[:, 19:20]).inc("s_aux")
                V.wait("s_aux", w.c("s_aux"))
                V.tensor_scalar(colsc[:, 20:21], colsc[:, 19:20], 1.0 / VD, float(EPS_P),
                                OP.mult, OP.add).inc("s_stat")
                S_.wait("s_stat", w.c("s_stat"))
                S_.activation(colsc[:, 21:22], colsc[:, 20:21], AF.Sqrt).inc("s_aux")
                V.wait("s_aux", w.c("s_aux"))
                V.reciprocal(colsc[:, 22:23], colsc[:, 21:22])
                V.drain()
                V.wait("s_ast", w.c("s_ast"))
                V.wait("s_a2", w.c("s_a2"))
                V.wait("g_st", w.c("g_st"))
                V.scalar_tensor_tensor(astg[0][:, :], osb[:, VSTR:VSTR + VD],
                                       colsc[:, 22:23], subl_at[:, :], OP.mult, OP.mult)
                ii = V.scalar_tensor_tensor(astg[1][:, :], osb[:, VSTR:VSTR + VD],
                                            colsc[:, 22:23], subl_bt[:, :], OP.mult, OP.mult)
                ii.inc("s_cmb")
                shard = 0 if qt < NTH else 1
                row = (qt % NTH) * 128
                Y.wait("s_cmb", w.c("s_cmb"))
                w.dma(Y, "s_ast", out=contrib[h].ap()[shard, row:row + 128, 0:256],
                      in_=astg[0][:, :])
                w.dma(Y, "s_a2", out=contrib[h].ap()[shard, row:row + 128, 256:512],
                      in_=astg[1][:, :])
                if last31 and qt == NT - 1:
                    w.dma(Y, "s_ast", out=a31_i.ap()[0:1, h * 256:(h + 1) * 256],
                          in_=astg[0][127:128, :])
                    w.dma(Y, "s_a2", out=a31_j.ap()[0:1, h * 256:(h + 1) * 256],
                          in_=astg[1][127:128, :])

        def row_matmul(M, nchk_, ngrp, wrows_fn, statT_fn, out_buf, bias_slice=None,
                       act=AF.Copy):
            for rnd in range((ngrp + 1) // 2):
                grps = [g for g in (2 * rnd, 2 * rnd + 1) if g < ngrp]
                for c in range(nchk_):
                    csem = "s_t1" if c % 2 == 0 else "s_t2"
                    mem["rowuse"] += 1
                    if mem["rowuse"] >= 3:
                        Y.wait("s_row", mem["rowuse"] - 2)
                    w.dma(Y, csem, out=thw[c % 2][:, 0:ngrp * 512], in_=wrows_fn(c))
                    T.wait(csem, w.c(csem))
                    if c == 0:
                        T.wait("s_ev", w.c("s_ev"))
                    lastmm = None
                    for gi, gg in enumerate(grps):
                        lastmm = T.matmul(ps[0:M, 3072 + gi * 512: 3072 + gi * 512 + 512],
                                          statT_fn(c), thw[c % 2][:, gg * 512:(gg + 1) * 512],
                                          start=(c == 0), stop=(c == nchk_ - 1 and bias_slice is None))
                        if c == nchk_ - 1 and bias_slice is not None:
                            lastmm = T.matmul(ps[0:M, 3072 + gi * 512: 3072 + gi * 512 + 512],
                                              ones1b[0:1, 0:M], bias_slice(gg),
                                              start=False, stop=True)
                    lastmm.inc("s_row")
                S_.wait("s_row", mem["rowuse"])
                for gi, gg in enumerate(grps):
                    ii = S_.activation(out_buf[0:M, gg * 512:(gg + 1) * 512],
                                       ps[0:M, 3072 + gi * 512: 3072 + gi * 512 + 512], act)
                    if gi == len(grps) - 1:
                        ii.inc("s_ev")

        def row_transpose(src_f32, nchk_):
            T.wait("s_ev", w.c("s_ev"))
            for c in range(nchk_):
                T.transpose(ps[:, 3072 + c * 4: 3072 + c * 4 + 4],
                            src_f32[0:4, c * 128:(c + 1) * 128], identf[0:4, 0:4]).inc("s_sc")
            V.wait("s_sc", w.c("s_sc"))
            V.tensor_copy(tT[:, 0:nchk_ * 4], ps[:, 3072: 3072 + nchk_ * 4]).inc("s_ev")
            return w.c("s_ev")

        def row_ln(P, buf, n_el, eps, sb_, gamma=None, beta=None):
            cs = lambda i: r4[0:P, sb_ + i: sb_ + i + 1]
            V.reduce_sum(cs(0), buf[0:P, 0:n_el], axis=AX.X).inc("s_stat")
            S_.wait("s_stat", w.c("s_stat"))
            S_.activation(rsq[0:P, 0:n_el], buf[0:P, 0:n_el], AF.Square,
                          accum_out=cs(1)).inc("s_aux")
            V.wait("s_aux", w.c("s_aux"))
            V.tensor_scalar(cs(0), cs(0), 1.0 / n_el, None, OP.mult)
            V.tensor_scalar(cs(1), cs(1), 1.0 / n_el, None, OP.mult)
            V.drain()
            V.tensor_scalar(cs(2), cs(0), cs(0), None, OP.mult)
            V.drain()
            V.tensor_tensor(cs(1), cs(1), cs(2), OP.subtract)
            V.drain()
            V.tensor_scalar(cs(1), cs(1), float(eps), None, OP.add).inc("s_stat")
            S_.wait("s_stat", w.c("s_stat"))
            S_.activation(cs(1), cs(1), AF.Sqrt).inc("s_aux")
            V.wait("s_aux", w.c("s_aux"))
            V.reciprocal(cs(2), cs(1))
            V.drain()
            V.tensor_scalar(cs(3), cs(0), cs(2), -1.0, OP.mult, OP.mult)
            V.drain()
            V.tensor_scalar(buf[0:P, 0:n_el], buf[0:P, 0:n_el], cs(2), cs(3),
                            OP.mult, OP.add)
            if gamma is not None:
                V.drain()
                V.tensor_tensor(buf[0:P, 0:n_el], buf[0:P, 0:n_el], gamma[0:P, 0:n_el], OP.mult)
                V.drain()
                V.tensor_tensor(buf[0:P, 0:n_el], buf[0:P, 0:n_el], beta[0:P, 0:n_el], OP.add)
            V.drain()

        V.wait("ini", 1)
        V.memset(vstrip[:, VD::VSTR], 1.0).inc("s_mk")
        for h in range(NHL):
            ldv = head_loads(h)
            qgroup(h, NQG - 1, ldv, last31=True)
        if DBG:
            w.dma(Y, "s_q4", out=dbg_id.ap()[3], in_=identb[:, :])
        G.wait("s_ast", w.c("s_ast"))
        G.wait("s_a2", w.c("s_a2"))
        G.collective_compute("AllGather", OP.bypass, replica_groups=PAIRS,
                             ins=[a31_i.ap().opt()], outs=[a31_o.ap().opt()]).inc("s_cc")
        G.collective_compute("AllGather", OP.bypass, replica_groups=PAIRS,
                             ins=[a31_j.ap().opt()], outs=[a31_p.ap().opt()]).inc("s_cc")
        ccv = w.c("s_cc")
        barrier()
        for ci, blk in enumerate(WO_PERM):
            Y.wait("s_cc", ccv)
            w.dma(Y, "s_qld", out=a4c[:, ci:ci + 1],
                  in_=a31_o.ap()[blk // 8: blk // 8 + 1,
                                 (blk % 8) * 128:(blk % 8) * 128 + 128],
                  allow_slow_non_contiguous=True)
            w.dma(Y, "s_qld", out=a4d[:, ci:ci + 1],
                  in_=a31_p.ap()[blk // 8: blk // 8 + 1,
                                 (blk % 8) * 128:(blk % 8) * 128 + 128],
                  allow_slow_non_contiguous=True)
        if DBG:
            w.dma(Y, "s_q4", out=dbg_id.ap()[7], in_=identb[:, :])
        V.wait("s_qld", w.c("s_qld"))
        V.tensor_tensor(a4c[:, :], a4c[:, :], a4d[:, :], OP.add).inc("s_stat")
        T.wait("s_stat", w.c("s_stat"))
        T.wait("s_qld", w.c("s_qld"))
        T.wait("s_w", WPRE)
        Y.wait("s_w", WPRE)
        row_matmul(1, NCHK, 4, lambda c: wo_bf.ap()[c * 128:(c + 1) * 128, :],
                   lambda c: a4c[:, c:c + 1], zrow)
        if DBG:
            Y.wait("s_ev", w.c("s_ev"))
            w.dma(Y, "s_q4", out=dbg_id.ap()[8], in_=identb[:, :])
        V.wait("s_ev", w.c("s_ev"))
        row_ln(1, zrow, D, EPS_P, 0)
        w.dma(Y, "s_x", out=xlr[0:1, :], in_=dr["x_last"])
        V.wait("s_x", w.c("s_x"))
        V.tensor_tensor(zrow[0:1, :], zrow[0:1, :], xlr[0:1, :], OP.add)
        V.drain()
        row_ln(1, zrow, D, EPS_P, 4)
        V.tensor_copy(zrow[0:1, :], zrow[0:1, :]).inc("s_stat")
        Y.wait("s_stat", w.c("s_stat"))
        w.dma(Y, "s_ast", out=z4_i.ap()[0:1, :], in_=zrow[0:1, :])
        if DBG:
            w.dma(Y, "s_q4", out=dbg_id.ap()[9], in_=identb[:, :])
        G.wait("s_ast", w.c("s_ast"))
        G.collective_compute("AllGather", OP.bypass, replica_groups=ALL8,
                             ins=[z4_i.ap().opt()], outs=[z4_o.ap().opt()]).inc("s_cc")
        ccv = w.c("s_cc")
        Y.wait("s_cc", ccv)
        w.dma(Y, "s_qld", out=throw[0:4, :], in_=z4_o.ap()[0:8:2, :])
        w.dma(Y, "s_qld", out=g4a[:, :], in_=dr["in_gamma"].broadcast_to([4, D]))
        w.dma(Y, "s_qld", out=g4b[:, :], in_=dr["in_beta"].broadcast_to([4, D]))
        w.dma(Y, "s_qld", out=d2bt[:, :], in_=dr["d2_b"].broadcast_to([4, D]))
        w.dma(Y, "s_qld", out=selb[:, :], in_=dr["sel_b"])
        V.wait("s_qld", w.c("s_qld"))
        row_ln(4, throw, D, EPS_A, 8, gamma=g4a, beta=g4b)
        V.memset(r4[0:4, 15:16], 0.0).inc("s_stat")
        if DBG:
            Y.wait("s_stat", w.c("s_stat"))
            w.dma(Y, "s_q4", out=dbg_id.ap()[10], in_=identb[:, :])
        T.wait("s_stat", w.c("s_stat"))
        row_transpose(throw, NCHK)
        w.dma(Y, "s_qld", out=brow[0:1, 0:D], in_=rb_bf.ap()[0:1, :])
        if DBG:
            Y.wait("s_ev", w.c("s_ev"))
            w.dma(Y, "s_q4", out=dbg_id.ap()[11], in_=identb[:, :])
        T.wait("s_qld", w.c("s_qld"))
        row_matmul(4, NCHK, 4, lambda c: agg_bf.ap()[c * 128:(c + 1) * 128, :],
                   lambda c: tT[:, c * 4:(c + 1) * 4], throw,
                   bias_slice=lambda gg: brow[0:1, gg * 512:(gg + 1) * 512])
        if DBG:
            w.dma(Y, "s_q4", out=dbg_id.ap()[4], in_=identb[:, :])
        Y.wait("s_ev", w.c("s_ev"))
        w.dma(Y, "s_qld", out=g4a[:, :], in_=dr["pn_gamma"].broadcast_to([4, D]))
        w.dma(Y, "s_qld", out=g4b[:, :], in_=dr["pn_beta"].broadcast_to([4, D]))
        pnv = w.c("s_qld")
        barrier()

        def thought_pre(k):
            V.wait("s_ev", w.c("s_ev"))
            V.wait("s_aux", w.c("s_aux"))
            V.wait("s_qld", pnv)
            V.tensor_copy(trow[0:4, :], throw[0:4, :])
            V.drain()
            row_ln(4, trow, D, EPS_A, 8, gamma=g4a, beta=g4b)
            V.memset(r4[0:4, 15:16], 0.0).inc("s_stat")
            T.wait("s_stat", w.c("s_stat"))
            row_transpose(trow, NCHK)
            Y.wait("s_row", mem["rowuse"])
            w.dma(Y, "s_qld", out=brow[0:1, 0:HFS], in_=rb_bf.ap()[3:4, 0:HFS])
            T.wait("s_qld", w.c("s_qld"))
            row_matmul(4, NCHK, 2, lambda c: d1_bf.ap()[c * 128:(c + 1) * 128, :],
                       lambda c: tT[:, c * 4:(c + 1) * 4], trow,
                       bias_slice=lambda gg: brow[0:1, gg * 512:(gg + 1) * 512], act=AF.Gelu)
            row_transpose(trow, HFS // 128)
            row_matmul(4, HFS // 128, 4, lambda c: d2_bf.ap()[c * 128:(c + 1) * 128, :],
                       lambda c: tT[:, c * 4:(c + 1) * 4], trow)
            V.wait("s_ev", w.c("s_ev"))
            V.tensor_copy(trow[0:4, :], trow[0:4, :]).inc("s_stat")
            Y.wait("s_stat", w.c("s_stat"))
            w.dma(Y, "s_ast", out=ar_i[k].ap()[0:4, :], in_=trow[0:4, :])
            G.wait("s_ast", w.c("s_ast"))
            G.collective_compute("AllReduce", OP.add, replica_groups=ALL8,
                                 ins=[ar_i[k].ap().opt()], outs=[ar_o[k].ap().opt()]).inc("s_cc")
            mem[f"ar{k}"] = w.c("s_cc")

        def thought_post(k):
            Y.wait("s_cc", mem[f"ar{k}"])
            w.dma(Y, "s_qld", out=trow[0:4, :], in_=ar_o[k].ap()[0:4, :])
            V.wait("s_qld", w.c("s_qld"))
            V.tensor_tensor(trow[0:4, :], trow[0:4, :], d2bt[0:4, :], OP.add)
            V.drain()
            V.tensor_tensor(throw[0:4, :], throw[0:4, :], trow[0:4, :], OP.add).inc("s_aux")

        def bc_final():
            V.wait("s_aux", w.c("s_aux"))
            T.wait("s_aux", w.c("s_aux"))
            row_transpose(throw, NCHK)
            Y.wait("s_row", mem["rowuse"])
            w.dma(Y, "s_qld", out=brow[0:1, 0:D], in_=rb_bf.ap()[1:2, :])
            T.wait("s_qld", w.c("s_qld"))
            row_matmul(4, NCHK, 4, lambda c: bcw_bf.ap()[c * 128:(c + 1) * 128, :],
                       lambda c: tT[:, c * 4:(c + 1) * 4], trow,
                       bias_slice=lambda gg: brow[0:1, gg * 512:(gg + 1) * 512])
            V.wait("s_ev", w.c("s_ev"))
            T.wait("s_ev", w.c("s_ev"))
            for og in range(4):
                pcol = 3072 + (og % 2) * 512
                if og >= 2:
                    T.wait("s_ev", w.c("s_ev") - 1)
                ii = T.matmul(ps[0:1, pcol: pcol + 512], selb[0:4, 0:1],
                              trow[0:4, og * 512:(og + 1) * 512], start=True, stop=True)
                ii.inc("s_sc")
                V.wait("s_sc", w.c("s_sc"))
                V.tensor_copy(zrow[0:1, og * 512:(og + 1) * 512],
                              ps[0:1, pcol: pcol + 512]).inc("s_ev")
            V.drain()
            V.tensor_copy(zrow[0:1, :], zrow[0:1, :]).inc("s_stat")
            Y.wait("s_stat", w.c("s_stat"))
            w.dma(Y, "s_ast", out=bc_scr.ap()[0:1, 0:D], in_=zrow[0:1, :])
            w.led["bc_ast"] = w.c("s_ast")

        for h in range(NHL):
            ldv = head_loads(h)
            qgroup(h, 0, ldv)
            if h > 0:
                thought_post(h - 1)
            thought_pre(h)
            for qg in range(1, NQG - 1):
                qgroup(h, qg, ldv)
            G.wait("s_ast", w.c("s_ast"))
            G.wait("s_a2", w.c("s_a2"))
            G.collective_compute("ReduceScatter", OP.add, replica_groups=PAIRS,
                                 ins=[contrib[h].ap().opt()],
                                 outs=[rs_out[h].ap().opt()]).inc("s_cc")
        thought_post(NHL - 1)
        barrier()
        bc_final()
        if DBG:
            w.dma(Y, "s_q4", out=dbg_id.ap()[6], in_=identb[:, :])
        w.fence()

    run_block(emit_a2)
    RS_DONE = seed[0]["s_cc"]
    TH_DONE = seed[0]["bc_ast"]
    close_last(25)

    wo_sb = sbuf("wo_sb", [128, NCHK * D], BF16)
    gw_sb = sbuf("gw_sb", [128, NCHK * D], BF16)
    a_sb = sbuf("a_sb", [128, D], BF16)
    aT = sbuf("aT", [128, D], BF16)
    zT = sbuf("zT", [128, D], BF16)
    gb_bf = sbuf("gb_bf", [1, D], BF16)
    bcB = sbuf("bcB", [128, D], F32)
    outgT = sbuf("outgT", [128, D], F32)
    outbT = sbuf("outbT", [128, D], F32)
    bA = sbuf("bA", [128, D], F32)
    bX = sbuf("bX", [128, D], F32)
    bC = sbuf("bC", [128, D], F32)
    bD = sbuf("bD", [128, D], F32)
    bE = sbuf("bE", [128, D], F32)
    FEN3 = seed[0]["s_fence"]
    DSEM3 = [(s, seed[0].get(s, 0)) for s in ["s_x", "s_x2", "s_w", "s_qld", "s_q2", "s_q3", "s_q4", "s_ast", "s_st2", "s_a2", "s_bld", "s_out", "s_cc", "g_ld", "g_st", "s_t1", "s_t2"]]

    def emit_b(w):
        T, V, S_, Y, G = w.T, w.V, w.S, w.Y, w.G
        w.block_start(FEN3, DSEM3)
        mem = {}
        Y.wait("s_w", WPRE)
        HC = NCHK // 2
        w.dma(Y, "s_bld", out=wo_sb[:, 0:HC * D].rearrange("p (c n) -> p c n", c=HC),
              in_=wo_bf.ap()[0:HC * 128, :].rearrange("(c p) n -> p c n", p=128))
        w.dma(Y, "s_st2", out=wo_sb[:, HC * D:].rearrange("p (c n) -> p c n", c=HC),
              in_=wo_bf.ap()[HC * 128:, :].rearrange("(c p) n -> p c n", p=128))
        w.dma(Y, "s_qld", out=gw_sb[:, 0:HC * D].rearrange("p (c n) -> p c n", c=HC),
              in_=gw_bf.ap()[0:HC * 128, :].rearrange("(c p) n -> p c n", p=128))
        w.dma(Y, "s_a2", out=gw_sb[:, HC * D:].rearrange("p (c n) -> p c n", c=HC),
              in_=gw_bf.ap()[HC * 128:, :].rearrange("(c p) n -> p c n", p=128))
        w.dma(Y, "s_q2", out=gb_bf[0:1, :], in_=rb_bf.ap()[2:3, :])
        Y.wait("s_ast", TH_DONE)
        w.dma(Y, "s_q2", out=bcB[:, :], in_=bc_scr.ap()[0:1, 0:D].broadcast_to([128, D]))
        w.dma(Y, "s_q2", out=outgT[:, :], in_=dr["out_gamma"].broadcast_to([128, D]))
        w.dma(Y, "s_q2", out=outbT[:, :], in_=dr["out_beta"].broadcast_to([128, D]))
        BLD0 = [("s_bld", w.c("s_bld")), ("s_st2", w.c("s_st2")),
                ("s_qld", w.c("s_qld")), ("s_a2", w.c("s_a2")), ("s_q2", w.c("s_q2"))]
        GB0 = ("s_q2", w.c("s_q2"))

        for tt in range(NTH):
            Y.wait("s_cc", RS_DONE)
            if tt > 0:
                Y.wait("s_tp", mem["a_tp"])
            for h in range(NHL):
                w.dma(Y, "s_q3" if h % 2 == 0 else "s_q4",
                      out=a_sb[:, h * 512:(h + 1) * 512],
                      in_=rs_out[h].ap()[tt * 128:(tt + 1) * 128, :])
            ldv = [("s_q3", w.c("s_q3")), ("s_q4", w.c("s_q4"))]
            if tt > 0:
                Y.wait("s_stat", mem["x1_done"])
            for xi, sxn in enumerate(["s_x", "s_x2", "s_t1", "s_t2"]):
                w.dma(Y, sxn, out=bX[:, xi * 512:(xi + 1) * 512],
                      in_=dr["x_half"][tt * 128:(tt + 1) * 128, xi * 512:(xi + 1) * 512])
            xldv = [(s_, w.c(s_)) for s_ in ["s_x", "s_x2", "s_t1", "s_t2"]]
            if tt > 0:
                Y.wait("s_bev", mem["bev"])
                w.dma(Y, "s_out", out=out_d[(tt - 1) * 128:tt * 128, :], in_=bD[:, :])
            if DBG and tt == 0:
                for s, v in ldv:
                    Y.wait(s, v)
                w.dma(Y, "s_q2", out=dbg_t.ap()[2], in_=a_sb[:, :])
                w.dma(Y, "s_q4", out=dbg_id.ap()[0], in_=identb[:, :])
                w.dma(Y, "s_q4", out=dbg_id.ap()[1], in_=identb[:, :])
            for s, v in BLD0 + ldv:
                T.wait(s, v)
            if tt > 0:
                T.wait("s_ev", w.c("s_ev"))
            for qd in range(4):
                for j in range(4):
                    c = qd * 4 + j
                    T.transpose(
                        ps[:, 2048 + qd * 512 + j * 64: 2048 + qd * 512 + (j + 1) * 64].bitcast(BF16),
                        a_sb[:, c * 128:(c + 1) * 128], identb[:, :]).inc("s_tp")
            mem["a_tp"] = w.c("s_tp")
            V.wait("s_tp", w.c("s_tp"))
            if tt > 0:
                V.wait("s_sc", mem["wo_sc"])
            if DBG:
                V.wait("s_q2", w.c("s_q2"))
            for qd in range(4):
                ii = V.tensor_copy(aT[:, qd * 512:(qd + 1) * 512],
                                   ps[:, 2048 + qd * 512: 2048 + qd * 512 + 256].bitcast(BF16))
                if qd == 3:
                    ii.inc("s_ev")
            atv = w.c("s_ev")
            if DBG and tt == 0:
                Y.wait("s_ev", atv)
                w.dma(Y, "s_q2", out=dbg_t.ap()[0], in_=aT[:, :])
            for og in range(4):
                T.wait("s_ev", atv)
                T.wait("s_bmm", w.c("s_bmm"))
                for c in range(NCHK):
                    ii = T.matmul(ps[:, og * 512: og * 512 + 512],
                                  aT[:, c * 128:(c + 1) * 128],
                                  wo_sb[:, c * D + og * 512: c * D + og * 512 + 512],
                                  start=(c == 0), stop=(c == NCHK - 1))
                    if c == NCHK - 1:
                        ii.inc("s_sc")
                S_.wait("s_sc", w.c("s_sc"))
                if og == 0:
                    S_.wait("s_stat", mem.get("x1_done", 0))
                    S_.wait("s_bev", tt)
                S_.activation(bA[:, og * 512:(og + 1) * 512], ps[:, og * 512: og * 512 + 512],
                              AF.Copy, accum_out=colsc[:, 16 + og:17 + og]).inc("s_bmm")
            mem["wo_sc"] = w.c("s_sc")
            if DBG and tt == 0:
                Y.wait("s_bmm", w.c("s_bmm"))
                w.dma(Y, "s_q2", out=dbg_b.ap()[0], in_=bA[:, :])
                w.dma(Y, "s_q2", out=dbg_b.ap()[5], in_=bX[:, :])
            V.wait("s_bmm", w.c("s_bmm"))
            V.reduce_sum(colsc[:, 0:1], colsc[:, 16:20], axis=AX.X)
            S_.wait("s_bmm", w.c("s_bmm"))
            S_.activation(bE[:, :], bA[:, :], AF.Square,
                          accum_out=colsc[:, 1:2]).inc("s_aux")
            ln_scalars(w, 0, D, EPS_P, pre_wait=("s_aux", w.c("s_aux")))
            for s_, v_ in xldv:
                V.wait(s_, v_)
            if DBG and tt == 0:
                V.wait("s_q2", w.c("s_q2"))
            V.tensor_scalar(bA[:, :], bA[:, :], colsc[:, 2:3], colsc[:, 3:4], OP.mult, OP.add)
            V.drain()
            V.scalar_tensor_tensor(bC[:, :], bA[:, :], 1.0, bX[:, :], OP.mult, OP.add,
                                   accum_out=colsc[:, 8:9]).inc("s_stat")
            mem["x1_done"] = w.c("s_stat")
            if DBG:
                S_.wait("s_q2", w.c("s_q2"))
            if DBG and tt == 0:
                Y.wait("s_stat", w.c("s_stat"))
                w.dma(Y, "s_q2", out=dbg_b.ap()[1], in_=bC[:, :])
                w.dma(Y, "s_q2", out=dbg_b.ap()[6][:, 0:32], in_=colsc[:, 0:32])
            S_.wait("s_stat", w.c("s_stat"))
            S_.activation(bA[:, :], bC[:, :], AF.Square, accum_out=colsc[:, 9:10]).inc("s_aux")
            ln_scalars(w, 8, D, EPS_P, pre_wait=("s_aux", w.c("s_aux")))
            if tt > 0:
                V.wait("s_tp", mem["z_tp"])
                V.wait("s_out", tt * 16)
            V.tensor_scalar(bD[:, :], bC[:, :], colsc[:, 10:11], colsc[:, 11:12],
                            OP.mult, OP.add).inc("s_stat")
            zv = w.c("s_stat")
            if DBG and tt == 0:
                Y.wait("s_stat", zv)
                w.dma(Y, "s_q2", out=dbg_b.ap()[2], in_=bD[:, :])
            T.wait("s_stat", zv)
            T.wait("s_ev", w.c("s_ev"))
            for qd in range(4):
                for j in range(4):
                    c = qd * 4 + j
                    T.transpose(ps[:, 2048 + qd * 512 + j * 128: 2048 + qd * 512 + (j + 1) * 128],
                                bD[:, c * 128:(c + 1) * 128], identf[:, :]).inc("s_tp")
            mem["z_tp"] = w.c("s_tp")
            V.wait("s_tp", w.c("s_tp"))
            V.wait("s_sc", mem.get("gt_sc", 0))
            if DBG:
                V.wait("s_q2", w.c("s_q2"))
            for qd in range(4):
                ii = V.tensor_copy(zT[:, qd * 512:(qd + 1) * 512],
                                   ps[:, 2048 + qd * 512: 2048 + qd * 512 + 512])
                if qd == 3:
                    ii.inc("s_ev")
            ztv = w.c("s_ev")
            if DBG and tt == 0:
                Y.wait("s_ev", ztv)
                w.dma(Y, "s_q2", out=dbg_t.ap()[1], in_=zT[:, :])
            for og in range(4):
                T.wait("s_ev", ztv)
                T.wait("s_bmm", w.c("s_bmm"))
                for c in range(NCHK):
                    T.matmul(ps[:, og * 512: og * 512 + 512],
                             zT[:, c * 128:(c + 1) * 128],
                             gw_sb[:, c * D + og * 512: c * D + og * 512 + 512],
                             start=(c == 0), stop=False)
                T.matmul(ps[:, og * 512: og * 512 + 512],
                         ones1b[0:1, 0:128], gb_bf[0:1, og * 512:(og + 1) * 512],
                         start=False, stop=True).inc("s_sc")
                S_.wait("s_sc", w.c("s_sc"))
                S_.activation(bE[:, og * 512:(og + 1) * 512], ps[:, og * 512: og * 512 + 512],
                              AF.Sigmoid).inc("s_bmm")
            mem["gt_sc"] = w.c("s_sc")
            if DBG and tt == 0:
                Y.wait("s_bmm", w.c("s_bmm"))
                w.dma(Y, "s_q2", out=dbg_b.ap()[3], in_=bE[:, :])
                w.dma(Y, "s_q2", out=dbg_b.ap()[7], in_=bcB[:, :])
            V.wait("s_bmm", w.c("s_bmm"))
            V.wait(*GB0)
            if DBG and tt == 0:
                V.wait("s_q2", w.c("s_q2"))
            V.tensor_tensor(bE[:, :], bE[:, :], bcB[:, :], OP.mult)
            V.drain()
            V.scalar_tensor_tensor(bE[:, :], bE[:, :], 1.0, bD[:, :], OP.mult, OP.add,
                                   accum_out=colsc[:, 0:1]).inc("s_stat")
            if DBG and tt == 0:
                Y.wait("s_stat", w.c("s_stat"))
                w.dma(Y, "s_q2", out=dbg_b.ap()[4], in_=bE[:, :])
            S_.wait("s_stat", w.c("s_stat"))
            S_.activation(bA[:, :], bE[:, :], AF.Square, accum_out=colsc[:, 1:2]).inc("s_aux")
            ln_scalars(w, 0, D, EPS_A, pre_wait=("s_aux", w.c("s_aux")))
            if DBG and tt == 0:
                V.wait("s_q2", w.c("s_q2"))
            V.tensor_scalar(bE[:, :], bE[:, :], colsc[:, 2:3], colsc[:, 3:4], OP.mult, OP.add)
            V.drain()
            V.tensor_tensor(bE[:, :], bE[:, :], outgT[:, :], OP.mult)
            V.drain()
            V.scalar_tensor_tensor(bE[:, :], bE[:, :], 1.0, outbT[:, :], OP.mult, OP.add,
                                   accum_out=colsc[:, 8:9]).inc("s_stat")
            S_.wait("s_stat", w.c("s_stat"))
            S_.activation(bA[:, :], bE[:, :], AF.Square, accum_out=colsc[:, 9:10]).inc("s_aux")
            ln_scalars(w, 8, D, EPS_P, pre_wait=("s_aux", w.c("s_aux")))
            V.tensor_scalar(bE[:, :], bE[:, :], colsc[:, 10:11], colsc[:, 11:12],
                            OP.mult, OP.add)
            V.drain()
            V.tensor_tensor(bD[:, :], bE[:, :], bC[:, :], OP.add).inc("s_bev")
            mem["bev"] = w.c("s_bev")
        Y.wait("s_bev", mem["bev"])
        w.dma(Y, "s_out", out=out_d[(NTH - 1) * 128:NTH * 128, :], in_=bD[:, :])
        if DBG:
            Y.wait("s_out", NTH * 16)
            w.dma(Y, "s_q3", out=dbg_ct.ap(), in_=contrib[0].ap())
            w.dma(Y, "s_q3", out=dbg_rs.ap(), in_=rs_out[0].ap())
            w.dma(Y, "s_q3", out=dbg_z4i.ap(), in_=z4_i.ap())
            w.dma(Y, "s_q3", out=dbg_z4o.ap(), in_=z4_o.ap())
            Y.wait("s_q3", w.c("s_q3"))
        Y.wait("s_out", NTH * 16)

    run_block(emit_b)
    n_inst = sum(1 for _ in nc.all_instructions())
    return nc, n_inst


_NC_CACHE = {}


def _get_nc(SS):
    if SS not in _NC_CACHE:
        _NC_CACHE[SS] = build_nc(SS)
    return _NC_CACHE[SS]


def kernel(**inputs):
    SS = int(np.asarray(inputs["L0_attn_x"]).shape[1])
    nc, _ = _get_nc(SS)
    in_maps = _stage(inputs, SS)
    res = run_bass_kernel_spmd(nc, in_maps, core_ids=list(range(8)))
    out = np.empty((B, SS, D), dtype=np.float32)
    for c in range(8):
        b, g = c // 2, c % 2
        out[b, g * (SS // 2):(g + 1) * (SS // 2)] = res.results[c]["out"]
    return out


# version 12 (final)
# speedup vs baseline: 1.0228x; 1.0228x over previous
import os
import numpy as np
import ml_dtypes
import concourse.bass as bass
import concourse.mybir as mybir
from concourse.bass_utils import run_bass_kernel_spmd

F32 = mybir.dt.float32
BF16 = mybir.dt.bfloat16
AF = mybir.ActivationFunctionType
OP = mybir.AluOpType
AX = mybir.AxisListType

B, D = 4, 2048
NH, HD, NHL = 8, 128, 4
SCALE = float(HD ** -0.5)
LAMBDA_INIT = 0.8 - 0.6 * float(np.exp(-0.3 * 12))
EPS_P = 1e-5
EPS_A = 1e-6
VD, VSTR = 256, 257
HF, HFS = 8192, 1024
KSTEPS = 4
NCHK = D // 128

WO_PERM = []
for _h in range(4):
    WO_PERM += [2 * _h, 2 * _h + 1, 8 + 2 * _h, 9 + 2 * _h]

PAIRS = [[0, 1], [2, 3], [4, 5], [6, 7]]
ALL8 = [list(range(8))]
ENGS = ("sync", "gpsimd", "scalar", "vector", "tensor")


class _Inst:
    __slots__ = ("w", "inst")

    def __init__(self, w, inst):
        self.w = w
        self.inst = inst

    def inc(self, sem, n=1):
        self.w.led[sem] = self.w.led.get(sem, 0) + n
        if self.inst is not None:
            self.inst.then_inc(self.w.sems[sem], n)
        return self


class _Eng:
    __slots__ = ("_w", "_name")

    def __init__(self, w, name):
        self._w = w
        self._name = name

    def wait(self, sem, val):
        if self._w.active == self._name and val > 0:
            self._w.eng.wait_ge(self._w.sems[sem], val)

    def drain(self):
        if self._w.active == self._name:
            self._w.eng.drain()

    def __getattr__(self, attr):
        w = self._w
        nm = self._name
        if w.active == nm:
            real = getattr(w.eng, attr)

            def call(*a, **k):
                ii = _Inst(w, real(*a, **k))
                w.last[nm] = ii
                w.last_kind[nm] = attr
                return ii
        else:
            def call(*a, **k):
                w.last_kind[nm] = attr
                return _Inst(w, None)
        return call


class Walk:
    def __init__(self, sems, active, eng, seed):
        self.sems = sems
        self.active = active
        self.eng = eng
        self.led = dict(seed)
        self.last = {}
        self.last_kind = {}
        self.T = _Eng(self, "tensor")
        self.V = _Eng(self, "vector")
        self.S = _Eng(self, "scalar")
        self.Y = _Eng(self, "sync")
        self.G = _Eng(self, "gpsimd")

    def c(self, sem):
        return self.led.get(sem, 0)

    def all_wait(self, sem, val):
        for e in (self.T, self.V, self.S, self.Y, self.G):
            e.wait(sem, val)

    def fence(self):
        for nm in ("tensor", "vector", "scalar"):
            if self.active == nm:
                self.eng.drain().then_inc(self.sems["s_fence"], 1)
        self.led["s_fence"] = self.led.get("s_fence", 0) + 3

    def block_start(self, fen, dma_sems):
        self.all_wait("s_fence", fen)
        for s, v in dma_sems:
            self.all_wait(s, v)

    def dma(self, E, sem, *a, **kw):
        E.wait(sem, self.c(sem))
        return E.dma_start(*a, **kw).inc(sem, 16)


def _stage(inp, SS):
    x = np.ascontiguousarray(np.asarray(inp["L0_attn_x"], dtype=np.float32))
    wq = np.asarray(inp["L0_attn_Wq"], dtype=np.float32)
    wk = np.asarray(inp["L0_attn_Wk"], dtype=np.float32)
    wv = np.asarray(inp["L0_attn_Wv"], dtype=np.float32)
    wo = np.asarray(inp["L0_attn_Wo"], dtype=np.float32)
    wo_perm = np.concatenate([wo[blk * 128:(blk + 1) * 128] for blk in WO_PERM], axis=0)
    subln = np.asarray(inp["L0_attn_subln_weight"], dtype=np.float32)
    zeros256 = np.zeros(256, dtype=np.float32)
    in_maps = []
    for c in range(8):
        b, g = c // 2, c % 2
        cols = slice(4 * g * 256, (4 * g + 4) * 256)
        m = {
            "x_full": x[b],
            "x_half": x[b, g * (SS // 2):(g + 1) * (SS // 2)],
            "x_last": x[b, SS - 1: SS],
            "wq": wq[:, cols], "wk": wk[:, cols], "wv": wv[:, cols],
            "wo_perm": wo_perm,
            "gate_w": np.asarray(inp["L0_ffn_gate_w"], dtype=np.float32),
            "agg_w": np.asarray(inp["L0_ffn_agg_w"], dtype=np.float32),
            "bc_w": np.asarray(inp["L0_ffn_bc_w"], dtype=np.float32),
            "d1s": np.asarray(inp["L0_ffn_d1_w"], dtype=np.float32)[:, c * HFS:(c + 1) * HFS],
            "d2s": np.asarray(inp["L0_ffn_d2_w"], dtype=np.float32)[c * HFS:(c + 1) * HFS, :],
            "d1bs": np.asarray(inp["L0_ffn_d1_b"], dtype=np.float32)[None, c * HFS:(c + 1) * HFS],
            "lamq1": np.asarray(inp["L0_attn_lambda_q1"], dtype=np.float32)[None, :],
            "lamk1": np.asarray(inp["L0_attn_lambda_k1"], dtype=np.float32)[None, :],
            "lamq2": np.asarray(inp["L0_attn_lambda_q2"], dtype=np.float32)[None, :],
            "lamk2": np.asarray(inp["L0_attn_lambda_k2"], dtype=np.float32)[None, :],
            "subl_a": (subln if g == 0 else zeros256)[None, :],
            "subl_b": (subln if g == 1 else zeros256)[None, :],
            "in_gamma": np.asarray(inp["L0_ffn_in_gamma"], dtype=np.float32)[None, :],
            "in_beta": np.asarray(inp["L0_ffn_in_beta"], dtype=np.float32)[None, :],
            "pn_gamma": np.asarray(inp["L0_ffn_pn_gamma"], dtype=np.float32)[None, :],
            "pn_beta": np.asarray(inp["L0_ffn_pn_beta"], dtype=np.float32)[None, :],
            "out_gamma": np.asarray(inp["L0_ffn_out_gamma"], dtype=np.float32)[None, :],
            "out_beta": np.asarray(inp["L0_ffn_out_beta"], dtype=np.float32)[None, :],
            "agg_b": np.asarray(inp["L0_ffn_agg_b"], dtype=np.float32)[None, :],
            "bc_b": np.asarray(inp["L0_ffn_bc_b"], dtype=np.float32)[None, :],
            "d2_b": np.asarray(inp["L0_ffn_d2_b"], dtype=np.float32)[None, :],
            "gate_b": np.asarray(inp["L0_ffn_gate_b"], dtype=np.float32)[None, :],
            "sel_b": np.eye(4, dtype=np.float32)[:, b:b + 1],
        }
        in_maps.append({k: np.ascontiguousarray(v) for k, v in m.items()})
    return in_maps


SEM_NAMES = ["ini", "s_x", "s_w", "s_stat", "s_aux", "s_ln", "s_tp", "s_ev", "s_mm",
             "s_sc", "s_ex", "s_mk", "s_av", "s_cmb", "s_ast", "s_cc", "s_th",
             "s_qld", "s_fence", "s_bld", "s_bmm", "s_bev", "s_out", "s_row",
             "g_ld", "g_st", "s_x2", "s_q2", "s_q3", "s_q4", "s_st2", "s_a2",
             "s_t1", "s_t2"]


def build_nc(SS):
    DBG = os.environ.get("KDBG") == "1"
    DK = "ExternalOutput" if DBG else "Internal"
    NT = SS // 128
    NTH = NT // 2
    NQG = SS // 512
    nc = bass.Bass("TRN2", target_bir_lowering=False, dynamic_dma_scratch_size=8192)

    dr = {}

    def din(name, shape):
        dr[name] = nc.dram_tensor(name, list(shape), F32, kind="ExternalInput").ap()

    din("x_full", (SS, D)); din("x_half", (SS // 2, D)); din("x_last", (1, D))
    din("wq", (D, 1024)); din("wk", (D, 1024)); din("wv", (D, 1024))
    din("wo_perm", (D, D)); din("gate_w", (D, D)); din("agg_w", (D, D)); din("bc_w", (D, D))
    din("d1s", (D, HFS)); din("d2s", (HFS, D)); din("d1bs", (1, HFS))
    for n in ("lamq1", "lamk1", "lamq2", "lamk2"):
        din(n, (1, 128))
    din("subl_a", (1, 256)); din("subl_b", (1, 256))
    for n in ("in_gamma", "in_beta", "pn_gamma", "pn_beta", "out_gamma", "out_beta",
              "agg_b", "bc_b", "d2_b", "gate_b"):
        din(n, (1, D))
    din("sel_b", (4, 1))
    out_d = nc.dram_tensor("out", [SS // 2, D], F32, kind="ExternalOutput").ap()

    q_scr = nc.dram_tensor("q_scr", [NHL, 2, 128, SS], BF16, kind=DK)
    k_scr = nc.dram_tensor("k_scr", [NHL, 2, 128, SS], BF16, kind="Internal")
    v_scr = nc.dram_tensor("v_scr", [NHL, NT, 128, VD], BF16, kind="Internal")
    contrib = [nc.dram_tensor(f"contrib{h}", [2, SS // 2, 512], BF16, kind="Internal")
               for h in range(NHL)]
    rs_out = [nc.dram_tensor(f"rs_out{h}", [SS // 2, 512], BF16, kind="Internal")
              for h in range(NHL)]
    a31_i = nc.dram_tensor("a31_i", [1, 1024], BF16, kind="Internal")
    a31_o = nc.dram_tensor("a31_o", [2, 1024], BF16, kind="Internal")
    a31_j = nc.dram_tensor("a31_j", [1, 1024], BF16, kind="Internal")
    a31_p = nc.dram_tensor("a31_p", [2, 1024], BF16, kind="Internal")
    z4_i = nc.dram_tensor("z4_i", [1, D], F32, kind="Internal")
    z4_o = nc.dram_tensor("z4_o", [8, D], F32, kind="Internal")
    ar_i = [nc.dram_tensor(f"ar_i{k}", [4, D], F32, kind="Internal") for k in range(KSTEPS)]
    ar_o = [nc.dram_tensor(f"ar_o{k}", [4, D], F32, kind="Internal") for k in range(KSTEPS)]
    bc_scr = nc.dram_tensor("bc_scr", [1, D + 8], F32, kind=DK)
    wo_bf = nc.dram_tensor("wo_bf", [D, D], BF16, kind="Internal")
    gw_bf = nc.dram_tensor("gw_bf", [D, D], BF16, kind="Internal")
    agg_bf = nc.dram_tensor("agg_bf", [D, D], BF16, kind="Internal")
    bcw_bf = nc.dram_tensor("bcw_bf", [D, D], BF16, kind="Internal")
    d1_bf = nc.dram_tensor("d1_bf", [D, HFS], BF16, kind="Internal")
    d2_bf = nc.dram_tensor("d2_bf", [HFS, D], BF16, kind="Internal")
    rb_bf = nc.dram_tensor("rb_bf", [4, D], BF16, kind="Internal")
    if DBG:
        dbg_b = nc.dram_tensor("dbg_b", [8, 128, D], F32, kind="ExternalOutput")
        dbg_t = nc.dram_tensor("dbg_t", [3, 128, D], BF16, kind="ExternalOutput")
        dbg_id = nc.dram_tensor("dbg_id", [14, 128, 128], BF16, kind="ExternalOutput")
        dbg_ct = nc.dram_tensor("dbg_ct", [2, SS // 2, 512], BF16, kind="ExternalOutput")
        dbg_rs = nc.dram_tensor("dbg_rs", [SS // 2, 512], BF16, kind="ExternalOutput")
        dbg_z4i = nc.dram_tensor("dbg_z4i", [1, D], F32, kind="ExternalOutput")
        dbg_z4o = nc.dram_tensor("dbg_z4o", [8, D], F32, kind="ExternalOutput")

    ctx_stack = []

    def sbuf(name, shape, dt):
        h = nc.sbuf_tensor(name, list(shape), dt)
        t = h.__enter__()
        ctx_stack.append(h)
        return t

    def close_last(n):
        for _ in range(n):
            ctx_stack.pop().__exit__(None, None, None)

    nc.allow_low_precision("bf16 matmul operands, fp32 accumulation").__enter__()
    ps = nc.psum_tensor("ps", [128, 4096], F32).__enter__()
    sems = {n: nc.semaphore(n).__enter__() for n in SEM_NAMES}

    identf = sbuf("identf", [128, 128], F32)
    identb = sbuf("identb", [128, 128], BF16)
    tri = sbuf("tri", [128, 128], BF16)
    subl_at = sbuf("subl_at", [128, 256], F32)
    subl_bt = sbuf("subl_bt", [128, 256], F32)
    neglam = sbuf("neglam", [128, 1], F32)
    ones1b = sbuf("ones1b", [1, 128], BF16)
    colsc = sbuf("colsc", [128, 32], F32)
    iop = sbuf("iop", [128, 1], F32)
    iof = sbuf("iof", [128, 128], F32)
    lrow = sbuf("lrow", [1, 512], F32)
    lsc = sbuf("lsc", [1, 8], F32)

    seed = [{}]

    def run_block(emitfn):
        w0 = Walk(sems, None, None, seed[0])
        emitfn(w0)
        with nc.Block() as block:
            for name in ENGS:
                getattr(block, name)(lambda e, n=name: emitfn(Walk(sems, n, e, seed[0])))
        seed[0] = w0.led

    def ln_scalars(w, base, n_el, eps, pre_wait=None):
        V, S_ = w.V, w.S
        cs = lambda i: colsc[:, base + i: base + i + 1]
        if pre_wait is not None:
            V.wait(*pre_wait)
        V.drain()
        V.tensor_scalar(cs(4), cs(0), 1.0 / n_el, None, OP.mult)
        V.tensor_scalar(cs(5), cs(1), 1.0 / n_el, None, OP.mult)
        V.drain()
        V.tensor_scalar(cs(1), cs(4), cs(4), None, OP.mult)
        V.drain()
        V.tensor_tensor(cs(5), cs(5), cs(1), OP.subtract)
        V.drain()
        V.tensor_scalar(cs(5), cs(5), float(eps), None, OP.add).inc("s_stat")
        S_.wait("s_stat", w.c("s_stat"))
        S_.activation(cs(5), cs(5), AF.Sqrt).inc("s_aux")
        V.wait("s_aux", w.c("s_aux"))
        V.reciprocal(cs(2), cs(5))
        V.drain()
        V.tensor_scalar(cs(3), cs(4), cs(2), -1.0, OP.mult, OP.mult).inc("s_stat")
        V.drain()
        return w.c("s_stat")

    def emit_prolog(w):
        T, V, S_, Y, G = w.T, w.V, w.S, w.Y, w.G
        G.iota(iop[:, :], pattern=[[0, 1]], channel_multiplier=1,
               allow_small_or_imprecise_dtypes=True)
        G.iota(iof[:, :], pattern=[[1, 128]], channel_multiplier=0,
               allow_small_or_imprecise_dtypes=True).inc("ini")
        for i, n in enumerate(("lamq1", "lamk1", "lamq2", "lamk2")):
            w.dma(G, "g_ld", out=lrow[0:1, i * 128:(i + 1) * 128], in_=dr[n])
        w.dma(Y, "s_qld", out=subl_at[:, :], in_=dr["subl_a"].broadcast_to([128, 256]))
        w.dma(Y, "s_qld", out=subl_bt[:, :], in_=dr["subl_b"].broadcast_to([128, 256]))
        ldv = w.c("g_ld")
        ldy = w.c("s_qld")
        for srcn, dst in (("wo_perm", wo_bf), ("gate_w", gw_bf), ("agg_w", agg_bf),
                          ("bc_w", bcw_bf), ("d1s", d1_bf), ("d2s", d2_bf)):
            w.dma(G, "s_w", out=dst.ap(), in_=dr[srcn])
        w.dma(G, "s_w", out=rb_bf.ap()[0:1, :], in_=dr["agg_b"])
        w.dma(G, "s_w", out=rb_bf.ap()[1:2, :], in_=dr["bc_b"])
        w.dma(G, "s_w", out=rb_bf.ap()[2:3, :], in_=dr["gate_b"])
        w.dma(G, "s_w", out=rb_bf.ap()[3:4, 0:HFS], in_=dr["d1bs"])
        V.wait("ini", 1)
        V.wait("g_ld", ldv)
        V.wait("s_qld", ldy)
        V.tensor_scalar(tri[:, :], iof[:, :], iop[:, :], None, OP.is_ge)
        V.tensor_scalar(identf[:, :], iof[:, :], iop[:, :], None, OP.is_equal)
        V.tensor_scalar(identb[:, :], iof[:, :], iop[:, :], None, OP.is_equal)
        V.memset(ones1b[:, :], 1.0)
        V.tensor_scalar(subl_at[:, :], subl_at[:, :], 1.0 - LAMBDA_INIT, None, OP.mult)
        V.tensor_scalar(subl_bt[:, :], subl_bt[:, :], 1.0 - LAMBDA_INIT, None, OP.mult)
        V.tensor_tensor(lrow[0:1, 0:128], lrow[0:1, 0:128], lrow[0:1, 128:256], OP.mult)
        V.tensor_tensor(lrow[0:1, 256:384], lrow[0:1, 256:384], lrow[0:1, 384:512], OP.mult)
        V.drain()
        V.reduce_sum(lsc[0:1, 0:1], lrow[0:1, 0:128], axis=AX.X)
        V.reduce_sum(lsc[0:1, 1:2], lrow[0:1, 256:384], axis=AX.X).inc("s_stat")
        S_.wait("s_stat", w.c("s_stat"))
        S_.activation(lsc[0:1, 2:3], lsc[0:1, 0:1], AF.Exp)
        S_.activation(lsc[0:1, 3:4], lsc[0:1, 1:2], AF.Exp).inc("s_aux")
        V.wait("s_aux", w.c("s_aux"))
        V.tensor_tensor(lsc[0:1, 4:5], lsc[0:1, 3:4], lsc[0:1, 2:3], OP.subtract)
        V.drain()
        V.tensor_scalar(lsc[0:1, 5:6], lsc[0:1, 4:5], -LAMBDA_INIT, None, OP.add).inc("s_stat")
        Y.wait("s_stat", w.c("s_stat"))
        w.dma(Y, "s_ast", out=bc_scr.ap()[0:1, D:D + 1], in_=lsc[0:1, 5:6],
              allow_slow_non_contiguous=True)
        Y.wait("s_ast", w.c("s_ast"))
        w.dma(Y, "s_x", out=neglam[:, :],
              in_=bc_scr.ap()[0:1, D:D + 1].broadcast_to([128, 1]),
              allow_slow_non_contiguous=True)
        w.fence()

    run_block(emit_prolog)
    close_last(4)
    WPRE = seed[0]["s_w"]

    lnT = sbuf("lnT", [128, NCHK * SS], BF16)
    xt = [sbuf(f"xt{i}", [128, D], F32) for i in range(2)]
    sqs = sbuf("sqs", [128, D], F32)
    lnt_t = [sbuf(f"lnt{i}", [128, D], BF16) for i in range(2)]
    FEN0 = seed[0]["s_fence"]
    DSEM0 = [(s, seed[0].get(s, 0)) for s in ["s_x", "s_x2", "s_w", "s_qld", "s_q2", "s_q3", "s_q4", "s_ast", "s_st2", "s_a2", "s_bld", "s_out", "s_cc", "g_ld", "g_st", "s_t1", "s_t2"]]

    def emit_a1a(w):
        T, V, S_, Y, G = w.T, w.V, w.S, w.Y, w.G
        w.block_start(FEN0, DSEM0)
        for t in range(NT):
            bi = t % 2
            sx = "s_x" if bi == 0 else "s_x2"
            sx2 = "s_q3" if bi == 0 else "s_q4"
            if t >= 2:
                Y.wait("s_ln", t - 1)
            w.dma(Y, sx, out=xt[bi][:, 0:1024], in_=dr["x_full"][t * 128:(t + 1) * 128, 0:1024])
            w.dma(Y, sx2, out=xt[bi][:, 1024:2048], in_=dr["x_full"][t * 128:(t + 1) * 128, 1024:2048])
            lv = [(sx, w.c(sx)), (sx2, w.c(sx2))]
            base = 8 * bi
            for s_, v_ in lv:
                V.wait(s_, v_)
            if t >= 2:
                V.wait("s_ln", t - 1)
            V.reduce_sum(colsc[:, base:base + 1], xt[bi][:, :], axis=AX.X)
            for s_, v_ in lv:
                S_.wait(s_, v_)
            S_.wait("s_aux", w.c("s_aux"))
            S_.activation(sqs[:, :], xt[bi][:, :], AF.Square,
                          accum_out=colsc[:, base + 1:base + 2]).inc("s_aux")
            ln_scalars(w, base, D, EPS_P, pre_wait=("s_aux", w.c("s_aux")))
            if t >= 2:
                V.wait("s_tp", 16 * (t - 1))
            V.tensor_scalar(lnt_t[bi][:, :], xt[bi][:, :],
                            colsc[:, base + 2:base + 3],
                            colsc[:, base + 3:base + 4], OP.mult, OP.add).inc("s_ln")
            lnv = w.c("s_ln")
            for qd in range(4):
                slot = qd % 2
                T.wait("s_ln", lnv)
                if t * 4 + qd >= 2:
                    T.wait("s_ev", w.c("s_ev") - 1)
                for j in range(4):
                    c = qd * 4 + j
                    T.transpose(
                        ps[:, slot * 512 + j * 64: slot * 512 + (j + 1) * 64].bitcast(BF16),
                        lnt_t[bi][:, c * 128:(c + 1) * 128], identb[:, :]).inc("s_tp")
                V.wait("s_tp", w.c("s_tp"))
                for j in range(4):
                    c = qd * 4 + j
                    ii = V.tensor_copy(lnT[:, c * SS + t * 128: c * SS + (t + 1) * 128],
                                       ps[:, slot * 512 + j * 64: slot * 512 + (j + 1) * 64].bitcast(BF16))
                    if j == 3:
                        ii.inc("s_ev")
        w.fence()

    run_block(emit_a1a)
    close_last(5)

    wsb = sbuf("wsb", [128, NCHK * 1024], BF16)
    stg = [sbuf(f"stg{i}", [128, 512], BF16) for i in range(2)]
    FEN1 = seed[0]["s_fence"]
    DSEM1 = [(s, seed[0].get(s, 0)) for s in ["s_x", "s_x2", "s_w", "s_qld", "s_q2", "s_q3", "s_q4", "s_ast", "s_st2", "s_a2", "s_bld", "s_out", "s_cc", "g_ld", "g_st", "s_t1", "s_t2"]]
    EV1 = seed[0]["s_ev"]

    def emit_a1b(w):
        T, V, S_, Y, G = w.T, w.V, w.S, w.Y, w.G
        w.block_start(FEN1, DSEM1)
        nmm = [0]

        def mm_unit(stats, movs, n_out, scr_ap, wv):
            slot = nmm[0] % 2
            ssem = "s_ast" if slot == 0 else "s_st2"
            T.wait("s_w", wv)
            if nmm[0] == 0:
                T.wait("s_ev", EV1)
            if nmm[0] >= 2:
                T.wait("s_mm", w.c("s_mm") - 1)
            for ci in range(NCHK):
                ii = T.matmul(ps[:, 2048 + slot * 512: 2048 + slot * 512 + n_out],
                              stats[ci], movs[ci], start=(ci == 0), stop=(ci == NCHK - 1))
                if ci == NCHK - 1:
                    ii.inc("s_sc")
            V.wait("s_sc", w.c("s_sc"))
            if nmm[0] >= 2:
                V.wait(ssem, w.c(ssem))
            V.tensor_copy(stg[slot][:, 0:n_out],
                          ps[:, 2048 + slot * 512: 2048 + slot * 512 + n_out]).inc("s_mm")
            Y.wait("s_mm", w.c("s_mm"))
            w.dma(Y, ssem, out=scr_ap, in_=stg[slot][:, 0:n_out])
            nmm[0] += 1

        for wname, scr in (("wq", q_scr), ("wk", k_scr)):
            G.wait("s_mm", w.c("s_mm"))
            w.dma(G, "s_w", out=wsb[:, :].rearrange("p (c n) -> p c n", c=NCHK),
                  in_=dr[wname].rearrange("(c p) n -> p c n", p=128))
            wv = w.c("s_w")
            for h in range(NHL):
                for br in range(2):
                    off = h * 256 + br * 128
                    for tg in range(SS // 512):
                        mm_unit(
                            [wsb[:, c * 1024 + off: c * 1024 + off + 128] for c in range(NCHK)],
                            [lnT[:, c * SS + tg * 512: c * SS + tg * 512 + 512] for c in range(NCHK)],
                            512, scr.ap()[h, br, :, tg * 512:(tg + 1) * 512], wv)
        G.wait("s_mm", w.c("s_mm"))
        w.dma(G, "s_w", out=wsb[:, :].rearrange("p (c n) -> p c n", c=NCHK),
              in_=dr["wv"].rearrange("(c p) n -> p c n", p=128))
        wv = w.c("s_w")
        for h in range(NHL):
            for t in range(NT):
                mm_unit(
                    [lnT[:, c * SS + t * 128: c * SS + (t + 1) * 128] for c in range(NCHK)],
                    [wsb[:, c * 1024 + h * 256: c * 1024 + h * 256 + VD] for c in range(NCHK)],
                    VD, v_scr.ap()[h, t, :, :], wv)
        w.fence()

    run_block(emit_a1b)
    close_last(4)
    AST_QKV = [("s_ast", seed[0].get("s_ast", 0)), ("s_st2", seed[0].get("s_st2", 0))]

    ksb = sbuf("ksb", [128, 2 * SS], BF16)
    qsb = sbuf("qsb", [128, 2 * SS], BF16)
    vstrip = sbuf("vstrip", [128, NT * VSTR], BF16)
    pts = [sbuf(f"pts{br}", [128, NT * 512], BF16) for br in range(2)]
    osb = sbuf("osb", [128, 3 * VSTR], F32)
    astg = [sbuf(f"astg{i}", [128, 256], BF16) for i in range(2)]
    thw = [sbuf(f"thw{i}", [128, 2048], BF16) for i in range(2)]
    throw = sbuf("throw", [4, D], F32)
    trow = sbuf("trow", [4, D], F32)
    rsq = sbuf("rsq", [4, D], F32)
    trow_b = sbuf("trow_b", [4, D], BF16)
    tT = sbuf("tT", [128, 64], BF16)
    selb = sbuf("selb", [4, 1], F32)
    r4 = sbuf("r4", [4, 16], F32)
    g4a = sbuf("g4a", [4, D], F32)
    g4b = sbuf("g4b", [4, D], F32)
    d2bt = sbuf("d2bt", [4, D], F32)
    zrow = sbuf("zrow", [1, D], F32)
    xlr = sbuf("xlr", [1, D], F32)
    brow = sbuf("brow", [1, 2048], BF16)
    a4c = sbuf("a4c", [128, 16], BF16)
    a4d = sbuf("a4d", [128, 16], BF16)
    FEN2 = seed[0]["s_fence"]
    DSEM2 = [(s, seed[0].get(s, 0)) for s in ["s_x", "s_x2", "s_w", "s_qld", "s_q2", "s_q3", "s_q4", "s_ast", "s_st2", "s_a2", "s_bld", "s_out", "s_cc", "g_ld", "g_st", "s_t1", "s_t2"]]

    def emit_a2(w):
        T, V, S_, Y, G = w.T, w.V, w.S, w.Y, w.G
        w.block_start(FEN2, DSEM2)

        def barrier():
            w.fence()
            fv = w.c("s_fence")
            w.all_wait("s_fence", fv)
            for s in ("s_x", "s_x2", "s_w", "s_qld", "s_q2", "s_q3", "s_q4", "s_ast",
                      "s_st2", "s_a2", "s_bld", "s_out", "s_cc", "g_ld", "g_st",
                      "s_t1", "s_t2"):
                w.all_wait(s, w.c(s))
        if DBG:
            w.dma(Y, "s_q4", out=dbg_id.ap()[2], in_=identb[:, :])
        mem = {"ast_astg": 0, "rowuse": 0}

        ROT = ("s_qld", "s_q2", "s_q3", "s_q4")

        def head_loads(h):
            Y.wait("s_ex", w.c("s_ex"))
            for s, v in AST_QKV:
                Y.wait(s, v)
            i = 0
            for br in range(2):
                w.dma(Y, ROT[i % 4], out=ksb[:, br * SS:(br + 1) * SS],
                      in_=k_scr.ap()[h, br, :, :]); i += 1
                w.dma(Y, ROT[i % 4], out=qsb[:, br * SS:(br + 1) * SS],
                      in_=q_scr.ap()[h, br, :, :]); i += 1
            Y.wait("s_av", w.c("s_av"))
            for t in range(NT):
                w.dma(Y, ROT[i % 4], out=vstrip[:, t * VSTR: t * VSTR + VD],
                      in_=v_scr.ap()[h, t, :, :]); i += 1
            return [(s, w.c(s)) for s in ROT]

        def qgroup(h, qg, ldv, last31=False):
            nkc = 4 * qg + 4
            nquad = (nkc + 3) // 4
            avv_prev = w.c("s_av")
            npair = (nkc + 1) // 2
            for br in range(2):
                for pair in range(npair):
                    slot = pair % 2
                    kcs = [pair * 2 + i for i in range(2) if pair * 2 + i < nkc]
                    if pair == 0:
                        for s, v in ldv:
                            T.wait(s, v)
                    T.wait("s_ex", w.c("s_ex") - 1 if pair >= 2 else w.c("s_ex"))
                    for i, kc in enumerate(kcs):
                        ii = T.matmul(ps[:, slot * 1024 + i * 512: slot * 1024 + i * 512 + 512],
                                      ksb[:, br * SS + kc * 128: br * SS + (kc + 1) * 128],
                                      qsb[:, br * SS + qg * 512: br * SS + qg * 512 + 512],
                                      start=True, stop=True)
                        if i == len(kcs) - 1:
                            ii.inc("s_sc")
                    S_.wait("s_sc", w.c("s_sc"))
                    S_.wait("s_av", avv_prev)
                    wid = len(kcs) * 512
                    S_.activation(pts[br][:, kcs[0] * 512: kcs[0] * 512 + wid],
                                  ps[:, slot * 1024: slot * 1024 + wid],
                                  AF.Exp, scale=SCALE).inc("s_ex")
                exv = w.c("s_ex")
                first = True
                last = None
                for t in range(4):
                    qt = 4 * qg + t
                    if first:
                        V.wait("s_ex", exv)
                        V.wait("s_av", avv_prev)
                        first = False
                    last = V.tensor_tensor(
                        pts[br][:, qt * 512 + t * 128: qt * 512 + (t + 1) * 128],
                        pts[br][:, qt * 512 + t * 128: qt * 512 + (t + 1) * 128],
                        tri[:, :], OP.mult)
                    for kc in range(qt + 1, nkc):
                        last = V.memset(pts[br][:, kc * 512 + t * 128: kc * 512 + (t + 1) * 128], 0.0)
                last.inc("s_mk")
            mkv = w.c("s_mk")
            for t in range(4):
                qt = 4 * qg + t
                for br in range(2):
                    obase = 2048 + br * 512
                    T.wait("s_mk", mkv)
                    T.wait("s_cmb", w.c("s_cmb"))
                    for kc in range(qt + 1):
                        ii = T.matmul(ps[:, obase: obase + VSTR],
                                      pts[br][:, kc * 512 + t * 128: kc * 512 + (t + 1) * 128],
                                      vstrip[:, kc * VSTR: kc * VSTR + VSTR],
                                      start=(kc == 0), stop=(kc == qt))
                        if kc == qt:
                            ii.inc("s_av")
                V.wait("s_av", w.c("s_av"))
                V.wait("s_aux", w.c("s_aux"))
                V.tensor_copy(osb[:, 0:VSTR], ps[:, 2048: 2048 + VSTR])
                V.tensor_copy(osb[:, VSTR: 2 * VSTR], ps[:, 2560: 2560 + VSTR])
                V.drain()
                V.reciprocal(colsc[:, 16:17], osb[:, VD:VD + 1])
                V.reciprocal(colsc[:, 17:18], osb[:, VSTR + VD:VSTR + VD + 1])
                V.drain()
                V.tensor_scalar(colsc[:, 18:19], colsc[:, 17:18], neglam[:, :], None, OP.mult)
                V.tensor_scalar(osb[:, 0:VD], osb[:, 0:VD], colsc[:, 16:17], None, OP.mult)
                V.drain()
                V.scalar_tensor_tensor(osb[:, VSTR:VSTR + VD], osb[:, VSTR:VSTR + VD],
                                       colsc[:, 18:19], osb[:, 0:VD], OP.mult, OP.add).inc("s_cmb")
                S_.wait("s_cmb", w.c("s_cmb"))
                S_.activation(osb[:, 2 * VSTR: 2 * VSTR + VD], osb[:, VSTR:VSTR + VD], AF.Square,
                              accum_out=colsc[:, 19:20]).inc("s_aux")
                V.wait("s_aux", w.c("s_aux"))
                V.tensor_scalar(colsc[:, 20:21], colsc[:, 19:20], 1.0 / VD, float(EPS_P),
                                OP.mult, OP.add).inc("s_stat")
                S_.wait("s_stat", w.c("s_stat"))
                S_.activation(colsc[:, 21:22], colsc[:, 20:21], AF.Sqrt).inc("s_aux")
                V.wait("s_aux", w.c("s_aux"))
                V.reciprocal(colsc[:, 22:23], colsc[:, 21:22])
                V.drain()
                V.wait("s_ast", w.c("s_ast"))
                V.wait("s_a2", w.c("s_a2"))
                V.wait("g_st", w.c("g_st"))
                V.scalar_tensor_tensor(astg[0][:, :], osb[:, VSTR:VSTR + VD],
                                       colsc[:, 22:23], subl_at[:, :], OP.mult, OP.mult)
                ii = V.scalar_tensor_tensor(astg[1][:, :], osb[:, VSTR:VSTR + VD],
                                            colsc[:, 22:23], subl_bt[:, :], OP.mult, OP.mult)
                ii.inc("s_cmb")
                shard = 0 if qt < NTH else 1
                row = (qt % NTH) * 128
                Y.wait("s_cmb", w.c("s_cmb"))
                w.dma(Y, "s_ast", out=contrib[h].ap()[shard, row:row + 128, 0:256],
                      in_=astg[0][:, :])
                w.dma(Y, "s_a2", out=contrib[h].ap()[shard, row:row + 128, 256:512],
                      in_=astg[1][:, :])
                if last31 and qt == NT - 1:
                    w.dma(Y, "s_ast", out=a31_i.ap()[0:1, h * 256:(h + 1) * 256],
                          in_=astg[0][127:128, :])
                    w.dma(Y, "s_a2", out=a31_j.ap()[0:1, h * 256:(h + 1) * 256],
                          in_=astg[1][127:128, :])

        def row_matmul(M, nchk_, ngrp, wrows_fn, statT_fn, out_buf, bias_slice=None,
                       act=AF.Copy):
            for rnd in range((ngrp + 1) // 2):
                grps = [g for g in (2 * rnd, 2 * rnd + 1) if g < ngrp]
                for c in range(nchk_):
                    csem = "s_t1" if c % 2 == 0 else "s_t2"
                    mem["rowuse"] += 1
                    if mem["rowuse"] >= 3:
                        Y.wait("s_row", mem["rowuse"] - 2)
                    w.dma(Y, csem, out=thw[c % 2][:, 0:ngrp * 512], in_=wrows_fn(c))
                    T.wait(csem, w.c(csem))
                    if c == 0:
                        T.wait("s_ev", w.c("s_ev"))
                    lastmm = None
                    for gi, gg in enumerate(grps):
                        lastmm = T.matmul(ps[0:M, 3072 + gi * 512: 3072 + gi * 512 + 512],
                                          statT_fn(c), thw[c % 2][:, gg * 512:(gg + 1) * 512],
                                          start=(c == 0), stop=(c == nchk_ - 1 and bias_slice is None))
                        if c == nchk_ - 1 and bias_slice is not None:
                            lastmm = T.matmul(ps[0:M, 3072 + gi * 512: 3072 + gi * 512 + 512],
                                              ones1b[0:1, 0:M], bias_slice(gg),
                                              start=False, stop=True)
                    lastmm.inc("s_row")
                S_.wait("s_row", mem["rowuse"])
                for gi, gg in enumerate(grps):
                    ii = S_.activation(out_buf[0:M, gg * 512:(gg + 1) * 512],
                                       ps[0:M, 3072 + gi * 512: 3072 + gi * 512 + 512], act)
                    if gi == len(grps) - 1:
                        ii.inc("s_ev")

        def row_transpose(src_f32, nchk_):
            T.wait("s_ev", w.c("s_ev"))
            for c in range(nchk_):
                T.transpose(ps[:, 3072 + c * 4: 3072 + c * 4 + 4],
                            src_f32[0:4, c * 128:(c + 1) * 128], identf[0:4, 0:4]).inc("s_sc")
            V.wait("s_sc", w.c("s_sc"))
            V.tensor_copy(tT[:, 0:nchk_ * 4], ps[:, 3072: 3072 + nchk_ * 4]).inc("s_ev")
            return w.c("s_ev")

        def row_ln(P, buf, n_el, eps, sb_, gamma=None, beta=None):
            cs = lambda i: r4[0:P, sb_ + i: sb_ + i + 1]
            V.reduce_sum(cs(0), buf[0:P, 0:n_el], axis=AX.X).inc("s_stat")
            S_.wait("s_stat", w.c("s_stat"))
            S_.wait("s_aux", w.c("s_aux"))
            S_.activation(rsq[0:P, 0:n_el], buf[0:P, 0:n_el], AF.Square,
                          accum_out=cs(1)).inc("s_aux")
            V.wait("s_aux", w.c("s_aux"))
            V.tensor_scalar(cs(0), cs(0), 1.0 / n_el, None, OP.mult)
            V.tensor_scalar(cs(1), cs(1), 1.0 / n_el, None, OP.mult)
            V.drain()
            V.tensor_scalar(cs(2), cs(0), cs(0), None, OP.mult)
            V.drain()
            V.tensor_tensor(cs(1), cs(1), cs(2), OP.subtract)
            V.drain()
            V.tensor_scalar(cs(1), cs(1), float(eps), None, OP.add).inc("s_stat")
            S_.wait("s_stat", w.c("s_stat"))
            S_.wait("s_aux", w.c("s_aux"))
            S_.activation(cs(1), cs(1), AF.Sqrt).inc("s_aux")
            V.wait("s_aux", w.c("s_aux"))
            V.reciprocal(cs(2), cs(1))
            V.drain()
            V.tensor_scalar(cs(3), cs(0), cs(2), -1.0, OP.mult, OP.mult)
            V.drain()
            V.tensor_scalar(buf[0:P, 0:n_el], buf[0:P, 0:n_el], cs(2), cs(3),
                            OP.mult, OP.add)
            if gamma is not None:
                V.drain()
                V.tensor_tensor(buf[0:P, 0:n_el], buf[0:P, 0:n_el], gamma[0:P, 0:n_el], OP.mult)
                V.drain()
                V.tensor_tensor(buf[0:P, 0:n_el], buf[0:P, 0:n_el], beta[0:P, 0:n_el], OP.add)
            V.drain()

        V.wait("ini", 1)
        V.memset(vstrip[:, VD::VSTR], 1.0).inc("s_mk")
        for h in range(NHL):
            ldv = head_loads(h)
            qgroup(h, NQG - 1, ldv, last31=True)
        if DBG:
            w.dma(Y, "s_q4", out=dbg_id.ap()[3], in_=identb[:, :])
        G.wait("s_ast", w.c("s_ast"))
        G.wait("s_a2", w.c("s_a2"))
        G.collective_compute("AllGather", OP.bypass, replica_groups=PAIRS,
                             ins=[a31_i.ap().opt()], outs=[a31_o.ap().opt()]).inc("s_cc")
        G.collective_compute("AllGather", OP.bypass, replica_groups=PAIRS,
                             ins=[a31_j.ap().opt()], outs=[a31_p.ap().opt()]).inc("s_cc")
        ccv = w.c("s_cc")
        ldv0 = head_loads(0)
        qgroup(0, 0, ldv0)
        if NQG - 1 > 1:
            qgroup(0, 1, ldv0)
        for ci, blk in enumerate(WO_PERM):
            Y.wait("s_cc", ccv)
            w.dma(Y, "s_qld" if ci % 2 == 0 else "s_q2", out=a4c[:, ci:ci + 1],
                  in_=a31_o.ap()[blk // 8: blk // 8 + 1,
                                 (blk % 8) * 128:(blk % 8) * 128 + 128],
                  allow_slow_non_contiguous=True)
            w.dma(Y, "s_q2" if ci % 2 == 0 else "s_qld", out=a4d[:, ci:ci + 1],
                  in_=a31_p.ap()[blk // 8: blk // 8 + 1,
                                 (blk % 8) * 128:(blk % 8) * 128 + 128],
                  allow_slow_non_contiguous=True)
        if DBG:
            w.dma(Y, "s_q4", out=dbg_id.ap()[7], in_=identb[:, :])
        V.wait("s_qld", w.c("s_qld"))
        V.wait("s_q2", w.c("s_q2"))
        V.tensor_tensor(a4c[:, :], a4c[:, :], a4d[:, :], OP.add).inc("s_stat")
        T.wait("s_stat", w.c("s_stat"))
        T.wait("s_qld", w.c("s_qld"))
        T.wait("s_q2", w.c("s_q2"))
        T.wait("s_w", WPRE)
        Y.wait("s_w", WPRE)
        row_matmul(1, NCHK, 4, lambda c: wo_bf.ap()[c * 128:(c + 1) * 128, :],
                   lambda c: a4c[:, c:c + 1], zrow)
        if DBG:
            Y.wait("s_ev", w.c("s_ev"))
            w.dma(Y, "s_q4", out=dbg_id.ap()[8], in_=identb[:, :])
        V.wait("s_ev", w.c("s_ev"))
        row_ln(1, zrow, D, EPS_P, 0)
        w.dma(Y, "s_x", out=xlr[0:1, :], in_=dr["x_last"])
        V.wait("s_x", w.c("s_x"))
        V.tensor_tensor(zrow[0:1, :], zrow[0:1, :], xlr[0:1, :], OP.add)
        V.drain()
        row_ln(1, zrow, D, EPS_P, 4)
        V.tensor_copy(zrow[0:1, :], zrow[0:1, :]).inc("s_stat")
        Y.wait("s_stat", w.c("s_stat"))
        w.dma(Y, "s_ast", out=z4_i.ap()[0:1, :], in_=zrow[0:1, :])
        if DBG:
            w.dma(Y, "s_q4", out=dbg_id.ap()[9], in_=identb[:, :])
        G.wait("s_ast", w.c("s_ast"))
        G.collective_compute("AllGather", OP.bypass, replica_groups=ALL8,
                             ins=[z4_i.ap().opt()], outs=[z4_o.ap().opt()]).inc("s_cc")
        ccv = w.c("s_cc")
        Y.wait("s_cc", ccv)
        w.dma(Y, "s_qld", out=throw[0:4, :], in_=z4_o.ap()[0:8:2, :])
        w.dma(Y, "s_qld", out=g4a[:, :], in_=dr["in_gamma"].broadcast_to([4, D]))
        w.dma(Y, "s_qld", out=g4b[:, :], in_=dr["in_beta"].broadcast_to([4, D]))
        w.dma(Y, "s_qld", out=d2bt[:, :], in_=dr["d2_b"].broadcast_to([4, D]))
        w.dma(Y, "s_qld", out=selb[:, :], in_=dr["sel_b"])
        V.wait("s_qld", w.c("s_qld"))
        row_ln(4, throw, D, EPS_A, 8, gamma=g4a, beta=g4b)
        V.memset(r4[0:4, 15:16], 0.0).inc("s_stat")
        if DBG:
            Y.wait("s_stat", w.c("s_stat"))
            w.dma(Y, "s_q4", out=dbg_id.ap()[10], in_=identb[:, :])
        T.wait("s_stat", w.c("s_stat"))
        row_transpose(throw, NCHK)
        w.dma(Y, "s_qld", out=brow[0:1, 0:D], in_=rb_bf.ap()[0:1, :])
        if DBG:
            Y.wait("s_ev", w.c("s_ev"))
            w.dma(Y, "s_q4", out=dbg_id.ap()[11], in_=identb[:, :])
        T.wait("s_qld", w.c("s_qld"))
        row_matmul(4, NCHK, 4, lambda c: agg_bf.ap()[c * 128:(c + 1) * 128, :],
                   lambda c: tT[:, c * 4:(c + 1) * 4], throw,
                   bias_slice=lambda gg: brow[0:1, gg * 512:(gg + 1) * 512])
        if DBG:
            w.dma(Y, "s_q4", out=dbg_id.ap()[4], in_=identb[:, :])
        Y.wait("s_ev", w.c("s_ev"))
        w.dma(Y, "s_qld", out=g4a[:, :], in_=dr["pn_gamma"].broadcast_to([4, D]))
        w.dma(Y, "s_qld", out=g4b[:, :], in_=dr["pn_beta"].broadcast_to([4, D]))
        pnv = w.c("s_qld")
        barrier()

        def thought_pre(k):
            V.wait("s_ev", w.c("s_ev"))
            V.wait("s_aux", w.c("s_aux"))
            V.wait("s_qld", pnv)
            V.tensor_copy(trow[0:4, :], throw[0:4, :])
            V.drain()
            row_ln(4, trow, D, EPS_A, 8, gamma=g4a, beta=g4b)
            V.memset(r4[0:4, 15:16], 0.0).inc("s_stat")
            T.wait("s_stat", w.c("s_stat"))
            row_transpose(trow, NCHK)
            Y.wait("s_row", mem["rowuse"])
            w.dma(Y, "s_qld", out=brow[0:1, 0:HFS], in_=rb_bf.ap()[3:4, 0:HFS])
            T.wait("s_qld", w.c("s_qld"))
            row_matmul(4, NCHK, 2, lambda c: d1_bf.ap()[c * 128:(c + 1) * 128, :],
                       lambda c: tT[:, c * 4:(c + 1) * 4], trow,
                       bias_slice=lambda gg: brow[0:1, gg * 512:(gg + 1) * 512], act=AF.Gelu)
            row_transpose(trow, HFS // 128)
            row_matmul(4, HFS // 128, 4, lambda c: d2_bf.ap()[c * 128:(c + 1) * 128, :],
                       lambda c: tT[:, c * 4:(c + 1) * 4], trow)
            V.wait("s_ev", w.c("s_ev"))
            V.tensor_copy(trow[0:4, :], trow[0:4, :]).inc("s_stat")
            Y.wait("s_stat", w.c("s_stat"))
            w.dma(Y, "s_ast", out=ar_i[k].ap()[0:4, :], in_=trow[0:4, :])
            G.wait("s_ast", w.c("s_ast"))
            G.collective_compute("AllReduce", OP.add, replica_groups=ALL8,
                                 ins=[ar_i[k].ap().opt()], outs=[ar_o[k].ap().opt()]).inc("s_cc")
            mem[f"ar{k}"] = w.c("s_cc")

        def thought_post(k):
            Y.wait("s_cc", mem[f"ar{k}"])
            w.dma(Y, "s_qld", out=trow[0:4, :], in_=ar_o[k].ap()[0:4, :])
            V.wait("s_qld", w.c("s_qld"))
            V.tensor_tensor(trow[0:4, :], trow[0:4, :], d2bt[0:4, :], OP.add)
            V.drain()
            V.tensor_tensor(throw[0:4, :], throw[0:4, :], trow[0:4, :], OP.add).inc("s_aux")

        def bc_final():
            V.wait("s_aux", w.c("s_aux"))
            T.wait("s_aux", w.c("s_aux"))
            row_transpose(throw, NCHK)
            Y.wait("s_row", mem["rowuse"])
            w.dma(Y, "s_qld", out=brow[0:1, 0:D], in_=rb_bf.ap()[1:2, :])
            T.wait("s_qld", w.c("s_qld"))
            row_matmul(4, NCHK, 4, lambda c: bcw_bf.ap()[c * 128:(c + 1) * 128, :],
                       lambda c: tT[:, c * 4:(c + 1) * 4], trow,
                       bias_slice=lambda gg: brow[0:1, gg * 512:(gg + 1) * 512])
            V.wait("s_ev", w.c("s_ev"))
            T.wait("s_ev", w.c("s_ev"))
            for og in range(4):
                pcol = 3072 + (og % 2) * 512
                if og >= 2:
                    T.wait("s_ev", w.c("s_ev") - 1)
                ii = T.matmul(ps[0:1, pcol: pcol + 512], selb[0:4, 0:1],
                              trow[0:4, og * 512:(og + 1) * 512], start=True, stop=True)
                ii.inc("s_sc")
                V.wait("s_sc", w.c("s_sc"))
                V.tensor_copy(zrow[0:1, og * 512:(og + 1) * 512],
                              ps[0:1, pcol: pcol + 512]).inc("s_ev")
            V.drain()
            V.tensor_copy(zrow[0:1, :], zrow[0:1, :]).inc("s_stat")
            Y.wait("s_stat", w.c("s_stat"))
            w.dma(Y, "s_ast", out=bc_scr.ap()[0:1, 0:D], in_=zrow[0:1, :])
            w.led["bc_ast"] = w.c("s_ast")

        for h in range(NHL):
            if h == 0:
                ldv = ldv0
                thought_pre(0)
                qstart = min(2, NQG - 1)
            else:
                ldv = head_loads(h)
                qgroup(h, 0, ldv)
                thought_post(h - 1)
                thought_pre(h)
                qstart = 1
            for qg in range(qstart, NQG - 1):
                qgroup(h, qg, ldv)
            G.wait("s_ast", w.c("s_ast"))
            G.wait("s_a2", w.c("s_a2"))
            G.collective_compute("ReduceScatter", OP.add, replica_groups=PAIRS,
                                 ins=[contrib[h].ap().opt()],
                                 outs=[rs_out[h].ap().opt()]).inc("s_cc")
        thought_post(NHL - 1)
        barrier()
        bc_final()
        if DBG:
            w.dma(Y, "s_q4", out=dbg_id.ap()[6], in_=identb[:, :])
        w.fence()

    run_block(emit_a2)
    RS_DONE = seed[0]["s_cc"]
    TH_DONE = seed[0]["bc_ast"]
    close_last(25)

    wo_sb = sbuf("wo_sb", [128, NCHK * D], BF16)
    gw_sb = sbuf("gw_sb", [128, NCHK * D], BF16)
    a_sb = sbuf("a_sb", [128, D], BF16)
    aT = sbuf("aT", [128, D], BF16)
    zT = sbuf("zT", [128, D], BF16)
    gb_bf = sbuf("gb_bf", [1, D], BF16)
    bcB = sbuf("bcB", [128, D], F32)
    outgT = sbuf("outgT", [128, D], F32)
    outbT = sbuf("outbT", [128, D], F32)
    bA = sbuf("bA", [128, D], F32)
    bX = sbuf("bX", [128, D], F32)
    bC = sbuf("bC", [128, D], F32)
    bD = sbuf("bD", [128, D], F32)
    bE = sbuf("bE", [128, D], F32)
    FEN3 = seed[0]["s_fence"]
    DSEM3 = [(s, seed[0].get(s, 0)) for s in ["s_x", "s_x2", "s_w", "s_qld", "s_q2", "s_q3", "s_q4", "s_ast", "s_st2", "s_a2", "s_bld", "s_out", "s_cc", "g_ld", "g_st", "s_t1", "s_t2"]]

    def emit_b(w):
        T, V, S_, Y, G = w.T, w.V, w.S, w.Y, w.G
        w.block_start(FEN3, DSEM3)
        mem = {}
        Y.wait("s_w", WPRE)
        HC = NCHK // 2
        w.dma(Y, "s_bld", out=wo_sb[:, 0:HC * D].rearrange("p (c n) -> p c n", c=HC),
              in_=wo_bf.ap()[0:HC * 128, :].rearrange("(c p) n -> p c n", p=128))
        w.dma(Y, "s_st2", out=wo_sb[:, HC * D:].rearrange("p (c n) -> p c n", c=HC),
              in_=wo_bf.ap()[HC * 128:, :].rearrange("(c p) n -> p c n", p=128))
        w.dma(Y, "s_qld", out=gw_sb[:, 0:HC * D].rearrange("p (c n) -> p c n", c=HC),
              in_=gw_bf.ap()[0:HC * 128, :].rearrange("(c p) n -> p c n", p=128))
        w.dma(Y, "s_a2", out=gw_sb[:, HC * D:].rearrange("p (c n) -> p c n", c=HC),
              in_=gw_bf.ap()[HC * 128:, :].rearrange("(c p) n -> p c n", p=128))
        w.dma(Y, "s_q2", out=gb_bf[0:1, :], in_=rb_bf.ap()[2:3, :])
        Y.wait("s_ast", TH_DONE)
        w.dma(Y, "s_q2", out=bcB[:, :], in_=bc_scr.ap()[0:1, 0:D].broadcast_to([128, D]))
        w.dma(Y, "s_q2", out=outgT[:, :], in_=dr["out_gamma"].broadcast_to([128, D]))
        w.dma(Y, "s_q2", out=outbT[:, :], in_=dr["out_beta"].broadcast_to([128, D]))
        BLD0 = [("s_bld", w.c("s_bld")), ("s_st2", w.c("s_st2")),
                ("s_qld", w.c("s_qld")), ("s_a2", w.c("s_a2")), ("s_q2", w.c("s_q2"))]
        GB0 = ("s_q2", w.c("s_q2"))

        for tt in range(NTH):
            Y.wait("s_cc", RS_DONE)
            if tt > 0:
                Y.wait("s_tp", mem["a_tp"])
            for h in range(NHL):
                w.dma(Y, "s_q3" if h % 2 == 0 else "s_q4",
                      out=a_sb[:, h * 512:(h + 1) * 512],
                      in_=rs_out[h].ap()[tt * 128:(tt + 1) * 128, :])
            ldv = [("s_q3", w.c("s_q3")), ("s_q4", w.c("s_q4"))]
            if tt > 0:
                Y.wait("s_stat", mem["x1_done"])
            for xi, sxn in enumerate(["s_x", "s_x2", "s_t1", "s_t2"]):
                w.dma(Y, sxn, out=bX[:, xi * 512:(xi + 1) * 512],
                      in_=dr["x_half"][tt * 128:(tt + 1) * 128, xi * 512:(xi + 1) * 512])
            xldv = [(s_, w.c(s_)) for s_ in ["s_x", "s_x2", "s_t1", "s_t2"]]
            if tt > 0:
                Y.wait("s_bev", mem["bev"])
                w.dma(Y, "s_out", out=out_d[(tt - 1) * 128:tt * 128, :], in_=bD[:, :])
            if DBG and tt == 0:
                for s, v in ldv:
                    Y.wait(s, v)
                w.dma(Y, "s_q2", out=dbg_t.ap()[2], in_=a_sb[:, :])
                w.dma(Y, "s_q4", out=dbg_id.ap()[0], in_=identb[:, :])
                w.dma(Y, "s_q4", out=dbg_id.ap()[1], in_=identb[:, :])
            for s, v in BLD0 + ldv:
                T.wait(s, v)
            if tt > 0:
                T.wait("s_ev", mem["ztv"])
            for qd in range(4):
                for j in range(4):
                    c = qd * 4 + j
                    T.transpose(
                        ps[:, 2048 + qd * 512 + j * 64: 2048 + qd * 512 + (j + 1) * 64].bitcast(BF16),
                        a_sb[:, c * 128:(c + 1) * 128], identb[:, :]).inc("s_tp")
            mem["a_tp"] = w.c("s_tp")
            if tt == 0:
                S_.wait("s_tp", w.c("s_tp"))
                for qd in range(4):
                    ii = S_.activation(aT[:, qd * 512:(qd + 1) * 512],
                                       ps[:, 2048 + qd * 512: 2048 + qd * 512 + 256].bitcast(BF16),
                                       AF.Copy)
                    if qd == 3:
                        ii.inc("s_ev")
                atv = w.c("s_ev")
            else:
                atv = mem["atv_next"]
            if DBG and tt == 0:
                Y.wait("s_ev", atv)
                w.dma(Y, "s_q2", out=dbg_t.ap()[0], in_=aT[:, :])
            for og in range(4):
                T.wait("s_ev", atv)
                T.wait("s_bmm", w.c("s_bmm"))
                for c in range(NCHK):
                    ii = T.matmul(ps[:, og * 512: og * 512 + 512],
                                  aT[:, c * 128:(c + 1) * 128],
                                  wo_sb[:, c * D + og * 512: c * D + og * 512 + 512],
                                  start=(c == 0), stop=(c == NCHK - 1))
                    if c == NCHK - 1:
                        ii.inc("s_sc")
                S_.wait("s_sc", w.c("s_sc"))
                if og == 0:
                    S_.wait("s_stat", mem.get("x1_done", 0))
                    S_.wait("s_bev", tt)
                S_.activation(bA[:, og * 512:(og + 1) * 512], ps[:, og * 512: og * 512 + 512],
                              AF.Copy, accum_out=colsc[:, 16 + og:17 + og]).inc("s_bmm")
            mem["wo_sc"] = w.c("s_sc")
            if DBG and tt == 0:
                Y.wait("s_bmm", w.c("s_bmm"))
                w.dma(Y, "s_q2", out=dbg_b.ap()[0], in_=bA[:, :])
                w.dma(Y, "s_q2", out=dbg_b.ap()[5], in_=bX[:, :])
            V.wait("s_bmm", w.c("s_bmm"))
            V.reduce_sum(colsc[:, 0:1], colsc[:, 16:20], axis=AX.X)
            S_.wait("s_bmm", w.c("s_bmm"))
            S_.activation(bE[:, :], bA[:, :], AF.Square,
                          accum_out=colsc[:, 1:2]).inc("s_aux")
            ln_scalars(w, 0, D, EPS_P, pre_wait=("s_aux", w.c("s_aux")))
            for s_, v_ in xldv:
                V.wait(s_, v_)
            if DBG and tt == 0:
                V.wait("s_q2", w.c("s_q2"))
            V.tensor_scalar(bA[:, :], bA[:, :], colsc[:, 2:3], colsc[:, 3:4], OP.mult, OP.add)
            V.drain()
            V.scalar_tensor_tensor(bC[:, :], bA[:, :], 1.0, bX[:, :], OP.mult, OP.add,
                                   accum_out=colsc[:, 8:9]).inc("s_stat")
            mem["x1_done"] = w.c("s_stat")
            if DBG:
                S_.wait("s_q2", w.c("s_q2"))
            if DBG and tt == 0:
                Y.wait("s_stat", w.c("s_stat"))
                w.dma(Y, "s_q2", out=dbg_b.ap()[1], in_=bC[:, :])
                w.dma(Y, "s_q2", out=dbg_b.ap()[6][:, 0:32], in_=colsc[:, 0:32])
            S_.wait("s_stat", w.c("s_stat"))
            S_.activation(bA[:, :], bC[:, :], AF.Square, accum_out=colsc[:, 9:10]).inc("s_aux")
            ln_scalars(w, 8, D, EPS_P, pre_wait=("s_aux", w.c("s_aux")))
            if tt > 0:
                V.wait("s_tp", mem["z_tp"])
                V.wait("s_out", tt * 16)
            V.tensor_scalar(bD[:, :], bC[:, :], colsc[:, 10:11], colsc[:, 11:12],
                            OP.mult, OP.add).inc("s_stat")
            zv = w.c("s_stat")
            if DBG and tt == 0:
                Y.wait("s_stat", zv)
                w.dma(Y, "s_q2", out=dbg_b.ap()[2], in_=bD[:, :])
            T.wait("s_stat", zv)
            T.wait("s_ev", w.c("s_ev"))
            for qd in range(4):
                for j in range(4):
                    c = qd * 4 + j
                    T.transpose(ps[:, 2048 + qd * 512 + j * 128: 2048 + qd * 512 + (j + 1) * 128],
                                bD[:, c * 128:(c + 1) * 128], identf[:, :]).inc("s_tp")
            mem["z_tp"] = w.c("s_tp")
            V.wait("s_tp", w.c("s_tp"))
            V.wait("s_sc", mem.get("gt_sc", 0))
            if DBG:
                V.wait("s_q2", w.c("s_q2"))
            for qd in range(4):
                ii = V.tensor_copy(zT[:, qd * 512:(qd + 1) * 512],
                                   ps[:, 2048 + qd * 512: 2048 + qd * 512 + 512])
                if qd == 3:
                    ii.inc("s_ev")
            ztv = w.c("s_ev")
            mem["ztv"] = ztv
            if DBG and tt == 0:
                Y.wait("s_ev", ztv)
                w.dma(Y, "s_q2", out=dbg_t.ap()[1], in_=zT[:, :])
            for og in range(4):
                T.wait("s_ev", ztv)
                T.wait("s_bmm", w.c("s_bmm"))
                for c in range(NCHK):
                    T.matmul(ps[:, og * 512: og * 512 + 512],
                             zT[:, c * 128:(c + 1) * 128],
                             gw_sb[:, c * D + og * 512: c * D + og * 512 + 512],
                             start=(c == 0), stop=False)
                T.matmul(ps[:, og * 512: og * 512 + 512],
                         ones1b[0:1, 0:128], gb_bf[0:1, og * 512:(og + 1) * 512],
                         start=False, stop=True).inc("s_sc")
                S_.wait("s_sc", w.c("s_sc"))
                S_.activation(bE[:, og * 512:(og + 1) * 512], ps[:, og * 512: og * 512 + 512],
                              AF.Sigmoid).inc("s_bmm")
            mem["gt_sc"] = w.c("s_sc")
            if tt + 1 < NTH:
                S_.wait("s_tp", w.c("s_tp") + 16)
                S_.wait("s_sc", mem["wo_sc"])
                for qd in range(4):
                    ii = S_.activation(aT[:, qd * 512:(qd + 1) * 512],
                                       ps[:, 2048 + qd * 512: 2048 + qd * 512 + 256].bitcast(BF16),
                                       AF.Copy)
                    if qd == 3:
                        ii.inc("s_ev")
                mem["atv_next"] = w.c("s_ev")
            if DBG and tt == 0:
                Y.wait("s_bmm", w.c("s_bmm"))
                w.dma(Y, "s_q2", out=dbg_b.ap()[3], in_=bE[:, :])
                w.dma(Y, "s_q2", out=dbg_b.ap()[7], in_=bcB[:, :])
            V.wait("s_bmm", w.c("s_bmm"))
            V.wait(*GB0)
            if DBG and tt == 0:
                V.wait("s_q2", w.c("s_q2"))
            V.tensor_tensor(bE[:, :], bE[:, :], bcB[:, :], OP.mult)
            V.drain()
            V.scalar_tensor_tensor(bE[:, :], bE[:, :], 1.0, bD[:, :], OP.mult, OP.add,
                                   accum_out=colsc[:, 0:1]).inc("s_stat")
            if DBG and tt == 0:
                Y.wait("s_stat", w.c("s_stat"))
                w.dma(Y, "s_q2", out=dbg_b.ap()[4], in_=bE[:, :])
            S_.wait("s_stat", w.c("s_stat"))
            S_.activation(bA[:, :], bE[:, :], AF.Square, accum_out=colsc[:, 1:2]).inc("s_aux")
            ln_scalars(w, 0, D, EPS_A, pre_wait=("s_aux", w.c("s_aux")))
            if DBG and tt == 0:
                V.wait("s_q2", w.c("s_q2"))
            V.tensor_scalar(bE[:, :], bE[:, :], colsc[:, 2:3], colsc[:, 3:4], OP.mult, OP.add)
            V.drain()
            V.tensor_tensor(bE[:, :], bE[:, :], outgT[:, :], OP.mult)
            V.drain()
            V.scalar_tensor_tensor(bE[:, :], bE[:, :], 1.0, outbT[:, :], OP.mult, OP.add,
                                   accum_out=colsc[:, 8:9]).inc("s_stat")
            S_.wait("s_stat", w.c("s_stat"))
            S_.activation(bA[:, :], bE[:, :], AF.Square, accum_out=colsc[:, 9:10]).inc("s_aux")
            ln_scalars(w, 8, D, EPS_P, pre_wait=("s_aux", w.c("s_aux")))
            V.tensor_scalar(bE[:, :], bE[:, :], colsc[:, 10:11], colsc[:, 11:12],
                            OP.mult, OP.add)
            V.drain()
            V.tensor_tensor(bD[:, :], bE[:, :], bC[:, :], OP.add).inc("s_bev")
            mem["bev"] = w.c("s_bev")
        Y.wait("s_bev", mem["bev"])
        w.dma(Y, "s_out", out=out_d[(NTH - 1) * 128:NTH * 128, :], in_=bD[:, :])
        if DBG:
            Y.wait("s_out", NTH * 16)
            w.dma(Y, "s_q3", out=dbg_ct.ap(), in_=contrib[0].ap())
            w.dma(Y, "s_q3", out=dbg_rs.ap(), in_=rs_out[0].ap())
            w.dma(Y, "s_q3", out=dbg_z4i.ap(), in_=z4_i.ap())
            w.dma(Y, "s_q3", out=dbg_z4o.ap(), in_=z4_o.ap())
            Y.wait("s_q3", w.c("s_q3"))
        Y.wait("s_out", NTH * 16)

    run_block(emit_b)
    n_inst = sum(1 for _ in nc.all_instructions())
    return nc, n_inst


_NC_CACHE = {}


def _get_nc(SS):
    if SS not in _NC_CACHE:
        _NC_CACHE[SS] = build_nc(SS)
    return _NC_CACHE[SS]


def kernel(**inputs):
    SS = int(np.asarray(inputs["L0_attn_x"]).shape[1])
    nc, _ = _get_nc(SS)
    in_maps = _stage(inputs, SS)
    res = run_bass_kernel_spmd(nc, in_maps, core_ids=list(range(8)))
    out = np.empty((B, SS, D), dtype=np.float32)
    for c in range(8):
        b, g = c // 2, c % 2
        out[b, g * (SS // 2):(g + 1) * (SS // 2)] = res.results[c]["out"]
    return out


# version 13
# speedup vs baseline: 1.0341x; 1.0110x over previous
import os
import numpy as np
import ml_dtypes
import concourse.bass as bass
import concourse.mybir as mybir
from concourse.bass_utils import run_bass_kernel_spmd

F32 = mybir.dt.float32
BF16 = mybir.dt.bfloat16
AF = mybir.ActivationFunctionType
OP = mybir.AluOpType
AX = mybir.AxisListType

B, D = 4, 2048
NH, HD, NHL = 8, 128, 4
SCALE = float(HD ** -0.5)
LAMBDA_INIT = 0.8 - 0.6 * float(np.exp(-0.3 * 12))
EPS_P = 1e-5
EPS_A = 1e-6
VD, VSTR = 256, 257
HF, HFS = 8192, 1024
KSTEPS = 4
NCHK = D // 128

WO_PERM = []
for _h in range(4):
    WO_PERM += [2 * _h, 2 * _h + 1, 8 + 2 * _h, 9 + 2 * _h]

PAIRS = [[0, 1], [2, 3], [4, 5], [6, 7]]
ALL8 = [list(range(8))]
ENGS = ("sync", "gpsimd", "scalar", "vector", "tensor")


class _Inst:
    __slots__ = ("w", "inst")

    def __init__(self, w, inst):
        self.w = w
        self.inst = inst

    def inc(self, sem, n=1):
        self.w.led[sem] = self.w.led.get(sem, 0) + n
        if self.inst is not None:
            self.inst.then_inc(self.w.sems[sem], n)
        return self


class _Eng:
    __slots__ = ("_w", "_name")

    def __init__(self, w, name):
        self._w = w
        self._name = name

    def wait(self, sem, val):
        if self._w.active == self._name and val > 0:
            self._w.eng.wait_ge(self._w.sems[sem], val)

    def drain(self):
        if self._w.active == self._name:
            self._w.eng.drain()

    def __getattr__(self, attr):
        w = self._w
        nm = self._name
        if w.active == nm:
            real = getattr(w.eng, attr)

            def call(*a, **k):
                ii = _Inst(w, real(*a, **k))
                w.last[nm] = ii
                w.last_kind[nm] = attr
                return ii
        else:
            def call(*a, **k):
                w.last_kind[nm] = attr
                return _Inst(w, None)
        return call


class Walk:
    def __init__(self, sems, active, eng, seed):
        self.sems = sems
        self.active = active
        self.eng = eng
        self.led = dict(seed)
        self.last = {}
        self.last_kind = {}
        self.T = _Eng(self, "tensor")
        self.V = _Eng(self, "vector")
        self.S = _Eng(self, "scalar")
        self.Y = _Eng(self, "sync")
        self.G = _Eng(self, "gpsimd")

    def c(self, sem):
        return self.led.get(sem, 0)

    def all_wait(self, sem, val):
        for e in (self.T, self.V, self.S, self.Y, self.G):
            e.wait(sem, val)

    def fence(self):
        for nm in ("tensor", "vector", "scalar"):
            if self.active == nm:
                self.eng.drain().then_inc(self.sems["s_fence"], 1)
        self.led["s_fence"] = self.led.get("s_fence", 0) + 3

    def block_start(self, fen, dma_sems):
        self.all_wait("s_fence", fen)
        for s, v in dma_sems:
            self.all_wait(s, v)

    def dma(self, E, sem, *a, **kw):
        E.wait(sem, self.c(sem))
        return E.dma_start(*a, **kw).inc(sem, 16)


def _stage(inp, SS):
    x = np.ascontiguousarray(np.asarray(inp["L0_attn_x"], dtype=np.float32))
    wq = np.asarray(inp["L0_attn_Wq"], dtype=np.float32)
    wk = np.asarray(inp["L0_attn_Wk"], dtype=np.float32)
    wv = np.asarray(inp["L0_attn_Wv"], dtype=np.float32)
    wo = np.asarray(inp["L0_attn_Wo"], dtype=np.float32)
    wo_perm = np.concatenate([wo[blk * 128:(blk + 1) * 128] for blk in WO_PERM], axis=0)
    subln = np.asarray(inp["L0_attn_subln_weight"], dtype=np.float32)
    zeros256 = np.zeros(256, dtype=np.float32)
    in_maps = []
    for c in range(8):
        b, g = c // 2, c % 2
        cols = slice(4 * g * 256, (4 * g + 4) * 256)
        m = {
            "x_full": x[b],
            "x_half": x[b, g * (SS // 2):(g + 1) * (SS // 2)],
            "x_last": x[b, SS - 1: SS],
            "wq": wq[:, cols], "wk": wk[:, cols], "wv": wv[:, cols],
            "wo_perm": wo_perm,
            "gate_w": np.asarray(inp["L0_ffn_gate_w"], dtype=np.float32),
            "agg_w": np.asarray(inp["L0_ffn_agg_w"], dtype=np.float32),
            "bc_w": np.asarray(inp["L0_ffn_bc_w"], dtype=np.float32),
            "d1s": np.asarray(inp["L0_ffn_d1_w"], dtype=np.float32)[:, c * HFS:(c + 1) * HFS],
            "d2s": np.asarray(inp["L0_ffn_d2_w"], dtype=np.float32)[c * HFS:(c + 1) * HFS, :],
            "d1bs": np.asarray(inp["L0_ffn_d1_b"], dtype=np.float32)[None, c * HFS:(c + 1) * HFS],
            "lamq1": np.asarray(inp["L0_attn_lambda_q1"], dtype=np.float32)[None, :],
            "lamk1": np.asarray(inp["L0_attn_lambda_k1"], dtype=np.float32)[None, :],
            "lamq2": np.asarray(inp["L0_attn_lambda_q2"], dtype=np.float32)[None, :],
            "lamk2": np.asarray(inp["L0_attn_lambda_k2"], dtype=np.float32)[None, :],
            "subl_a": (subln if g == 0 else zeros256)[None, :],
            "subl_b": (subln if g == 1 else zeros256)[None, :],
            "in_gamma": np.asarray(inp["L0_ffn_in_gamma"], dtype=np.float32)[None, :],
            "in_beta": np.asarray(inp["L0_ffn_in_beta"], dtype=np.float32)[None, :],
            "pn_gamma": np.asarray(inp["L0_ffn_pn_gamma"], dtype=np.float32)[None, :],
            "pn_beta": np.asarray(inp["L0_ffn_pn_beta"], dtype=np.float32)[None, :],
            "out_gamma": np.asarray(inp["L0_ffn_out_gamma"], dtype=np.float32)[None, :],
            "out_beta": np.asarray(inp["L0_ffn_out_beta"], dtype=np.float32)[None, :],
            "agg_b": np.asarray(inp["L0_ffn_agg_b"], dtype=np.float32)[None, :],
            "bc_b": np.asarray(inp["L0_ffn_bc_b"], dtype=np.float32)[None, :],
            "d2_b": np.asarray(inp["L0_ffn_d2_b"], dtype=np.float32)[None, :],
            "gate_b": np.asarray(inp["L0_ffn_gate_b"], dtype=np.float32)[None, :],
            "sel_b": np.eye(4, dtype=np.float32)[:, b:b + 1],
        }
        in_maps.append({k: np.ascontiguousarray(v) for k, v in m.items()})
    return in_maps


SEM_NAMES = ["ini", "s_x", "s_w", "s_stat", "s_aux", "s_ln", "s_tp", "s_ev", "s_mm",
             "s_sc", "s_ex", "s_mk", "s_av", "s_cmb", "s_ast", "s_cc", "s_th",
             "s_qld", "s_fence", "s_bld", "s_bmm", "s_bev", "s_out", "s_row",
             "g_ld", "g_st", "s_x2", "s_q2", "s_q3", "s_q4", "s_st2", "s_a2",
             "s_t1", "s_t2"]


def build_nc(SS):
    DBG = os.environ.get("KDBG") == "1"
    DK = "ExternalOutput" if DBG else "Internal"
    NT = SS // 128
    NTH = NT // 2
    NQG = SS // 512
    nc = bass.Bass("TRN2", target_bir_lowering=False, dynamic_dma_scratch_size=8192)

    dr = {}

    def din(name, shape):
        dr[name] = nc.dram_tensor(name, list(shape), F32, kind="ExternalInput").ap()

    din("x_full", (SS, D)); din("x_half", (SS // 2, D)); din("x_last", (1, D))
    din("wq", (D, 1024)); din("wk", (D, 1024)); din("wv", (D, 1024))
    din("wo_perm", (D, D)); din("gate_w", (D, D)); din("agg_w", (D, D)); din("bc_w", (D, D))
    din("d1s", (D, HFS)); din("d2s", (HFS, D)); din("d1bs", (1, HFS))
    for n in ("lamq1", "lamk1", "lamq2", "lamk2"):
        din(n, (1, 128))
    din("subl_a", (1, 256)); din("subl_b", (1, 256))
    for n in ("in_gamma", "in_beta", "pn_gamma", "pn_beta", "out_gamma", "out_beta",
              "agg_b", "bc_b", "d2_b", "gate_b"):
        din(n, (1, D))
    din("sel_b", (4, 1))
    out_d = nc.dram_tensor("out", [SS // 2, D], F32, kind="ExternalOutput").ap()

    q_scr = nc.dram_tensor("q_scr", [NHL, 2, 128, SS], BF16, kind=DK)
    k_scr = nc.dram_tensor("k_scr", [NHL, 2, 128, SS], BF16, kind="Internal")
    v_scr = nc.dram_tensor("v_scr", [NHL, NT, 128, VD], BF16, kind="Internal")
    contrib = [nc.dram_tensor(f"contrib{h}", [2, SS // 2, 512], BF16, kind="Internal")
               for h in range(NHL)]
    rs_out = [nc.dram_tensor(f"rs_out{h}", [SS // 2, 512], BF16, kind="Internal")
              for h in range(NHL)]
    a31_i = nc.dram_tensor("a31_i", [1, 1024], BF16, kind="Internal")
    a31_o = nc.dram_tensor("a31_o", [2, 1024], BF16, kind="Internal")
    a31_j = nc.dram_tensor("a31_j", [1, 1024], BF16, kind="Internal")
    a31_p = nc.dram_tensor("a31_p", [2, 1024], BF16, kind="Internal")
    z4_i = nc.dram_tensor("z4_i", [1, D], F32, kind="Internal")
    z4_o = nc.dram_tensor("z4_o", [8, D], F32, kind="Internal")
    ar_i = [nc.dram_tensor(f"ar_i{k}", [4, D], F32, kind="Internal") for k in range(KSTEPS)]
    ar_o = [nc.dram_tensor(f"ar_o{k}", [4, D], F32, kind="Internal") for k in range(KSTEPS)]
    bc_scr = nc.dram_tensor("bc_scr", [1, D + 8], F32, kind=DK)
    wo_bf = nc.dram_tensor("wo_bf", [D, D], BF16, kind="Internal")
    gw_bf = nc.dram_tensor("gw_bf", [D, D], BF16, kind="Internal")
    agg_bf = nc.dram_tensor("agg_bf", [D, D], BF16, kind="Internal")
    bcw_bf = nc.dram_tensor("bcw_bf", [D, D], BF16, kind="Internal")
    d1_bf = nc.dram_tensor("d1_bf", [D, HFS], BF16, kind="Internal")
    d2_bf = nc.dram_tensor("d2_bf", [HFS, D], BF16, kind="Internal")
    rb_bf = nc.dram_tensor("rb_bf", [4, D], BF16, kind="Internal")
    if DBG:
        dbg_b = nc.dram_tensor("dbg_b", [8, 128, D], F32, kind="ExternalOutput")
        dbg_t = nc.dram_tensor("dbg_t", [3, 128, D], BF16, kind="ExternalOutput")
        dbg_id = nc.dram_tensor("dbg_id", [14, 128, 128], BF16, kind="ExternalOutput")
        dbg_ct = nc.dram_tensor("dbg_ct", [2, SS // 2, 512], BF16, kind="ExternalOutput")
        dbg_rs = nc.dram_tensor("dbg_rs", [SS // 2, 512], BF16, kind="ExternalOutput")
        dbg_z4i = nc.dram_tensor("dbg_z4i", [1, D], F32, kind="ExternalOutput")
        dbg_z4o = nc.dram_tensor("dbg_z4o", [8, D], F32, kind="ExternalOutput")

    ctx_stack = []

    def sbuf(name, shape, dt):
        h = nc.sbuf_tensor(name, list(shape), dt)
        t = h.__enter__()
        ctx_stack.append(h)
        return t

    def close_last(n):
        for _ in range(n):
            ctx_stack.pop().__exit__(None, None, None)

    nc.allow_low_precision("bf16 matmul operands, fp32 accumulation").__enter__()
    ps = nc.psum_tensor("ps", [128, 4096], F32).__enter__()
    sems = {n: nc.semaphore(n).__enter__() for n in SEM_NAMES}

    identf = sbuf("identf", [128, 128], F32)
    identb = sbuf("identb", [128, 128], BF16)
    tri = sbuf("tri", [128, 128], BF16)
    subl_at = sbuf("subl_at", [128, 256], F32)
    subl_bt = sbuf("subl_bt", [128, 256], F32)
    neglam = sbuf("neglam", [128, 1], F32)
    ones1b = sbuf("ones1b", [1, 128], BF16)
    colsc = sbuf("colsc", [128, 32], F32)
    iop = sbuf("iop", [128, 1], F32)
    iof = sbuf("iof", [128, 128], F32)
    lrow = sbuf("lrow", [1, 512], F32)
    lsc = sbuf("lsc", [1, 8], F32)

    seed = [{}]

    def run_block(emitfn):
        w0 = Walk(sems, None, None, seed[0])
        emitfn(w0)
        with nc.Block() as block:
            for name in ENGS:
                getattr(block, name)(lambda e, n=name: emitfn(Walk(sems, n, e, seed[0])))
        seed[0] = w0.led

    def ln_scalars(w, base, n_el, eps, pre_wait=None):
        V, S_ = w.V, w.S
        cs = lambda i: colsc[:, base + i: base + i + 1]
        if pre_wait is not None:
            V.wait(*pre_wait)
        V.drain()
        V.tensor_scalar(cs(4), cs(0), 1.0 / n_el, None, OP.mult)
        V.tensor_scalar(cs(5), cs(1), 1.0 / n_el, None, OP.mult)
        V.drain()
        V.tensor_scalar(cs(1), cs(4), cs(4), None, OP.mult)
        V.drain()
        V.tensor_tensor(cs(5), cs(5), cs(1), OP.subtract)
        V.drain()
        V.tensor_scalar(cs(5), cs(5), float(eps), None, OP.add).inc("s_stat")
        S_.wait("s_stat", w.c("s_stat"))
        S_.activation(cs(5), cs(5), AF.Sqrt).inc("s_aux")
        V.wait("s_aux", w.c("s_aux"))
        V.reciprocal(cs(2), cs(5))
        V.drain()
        V.tensor_scalar(cs(3), cs(4), cs(2), -1.0, OP.mult, OP.mult).inc("s_stat")
        V.drain()
        return w.c("s_stat")

    def emit_prolog(w):
        T, V, S_, Y, G = w.T, w.V, w.S, w.Y, w.G
        G.iota(iop[:, :], pattern=[[0, 1]], channel_multiplier=1,
               allow_small_or_imprecise_dtypes=True)
        G.iota(iof[:, :], pattern=[[1, 128]], channel_multiplier=0,
               allow_small_or_imprecise_dtypes=True).inc("ini")
        for i, n in enumerate(("lamq1", "lamk1", "lamq2", "lamk2")):
            w.dma(G, "g_ld", out=lrow[0:1, i * 128:(i + 1) * 128], in_=dr[n])
        w.dma(Y, "s_qld", out=subl_at[:, :], in_=dr["subl_a"].broadcast_to([128, 256]))
        w.dma(Y, "s_qld", out=subl_bt[:, :], in_=dr["subl_b"].broadcast_to([128, 256]))
        ldv = w.c("g_ld")
        ldy = w.c("s_qld")
        for srcn, dst in (("wo_perm", wo_bf), ("gate_w", gw_bf), ("agg_w", agg_bf),
                          ("bc_w", bcw_bf), ("d1s", d1_bf), ("d2s", d2_bf)):
            w.dma(G, "s_w", out=dst.ap(), in_=dr[srcn])
        w.dma(G, "s_w", out=rb_bf.ap()[0:1, :], in_=dr["agg_b"])
        w.dma(G, "s_w", out=rb_bf.ap()[1:2, :], in_=dr["bc_b"])
        w.dma(G, "s_w", out=rb_bf.ap()[2:3, :], in_=dr["gate_b"])
        w.dma(G, "s_w", out=rb_bf.ap()[3:4, 0:HFS], in_=dr["d1bs"])
        V.wait("ini", 1)
        V.wait("g_ld", ldv)
        V.wait("s_qld", ldy)
        V.tensor_scalar(tri[:, :], iof[:, :], iop[:, :], None, OP.is_ge)
        V.tensor_scalar(identf[:, :], iof[:, :], iop[:, :], None, OP.is_equal)
        V.tensor_scalar(identb[:, :], iof[:, :], iop[:, :], None, OP.is_equal)
        V.memset(ones1b[:, :], 1.0)
        V.tensor_scalar(subl_at[:, :], subl_at[:, :], 1.0 - LAMBDA_INIT, None, OP.mult)
        V.tensor_scalar(subl_bt[:, :], subl_bt[:, :], 1.0 - LAMBDA_INIT, None, OP.mult)
        V.tensor_tensor(lrow[0:1, 0:128], lrow[0:1, 0:128], lrow[0:1, 128:256], OP.mult)
        V.tensor_tensor(lrow[0:1, 256:384], lrow[0:1, 256:384], lrow[0:1, 384:512], OP.mult)
        V.drain()
        V.reduce_sum(lsc[0:1, 0:1], lrow[0:1, 0:128], axis=AX.X)
        V.reduce_sum(lsc[0:1, 1:2], lrow[0:1, 256:384], axis=AX.X).inc("s_stat")
        S_.wait("s_stat", w.c("s_stat"))
        S_.activation(lsc[0:1, 2:3], lsc[0:1, 0:1], AF.Exp)
        S_.activation(lsc[0:1, 3:4], lsc[0:1, 1:2], AF.Exp).inc("s_aux")
        V.wait("s_aux", w.c("s_aux"))
        V.tensor_tensor(lsc[0:1, 4:5], lsc[0:1, 3:4], lsc[0:1, 2:3], OP.subtract)
        V.drain()
        V.tensor_scalar(lsc[0:1, 5:6], lsc[0:1, 4:5], -LAMBDA_INIT, None, OP.add).inc("s_stat")
        Y.wait("s_stat", w.c("s_stat"))
        w.dma(Y, "s_ast", out=bc_scr.ap()[0:1, D:D + 1], in_=lsc[0:1, 5:6],
              allow_slow_non_contiguous=True)
        Y.wait("s_ast", w.c("s_ast"))
        w.dma(Y, "s_x", out=neglam[:, :],
              in_=bc_scr.ap()[0:1, D:D + 1].broadcast_to([128, 1]),
              allow_slow_non_contiguous=True)
        w.fence()

    run_block(emit_prolog)
    close_last(4)
    WPRE = seed[0]["s_w"]

    lnT = sbuf("lnT", [128, NCHK * SS], BF16)
    xt = [sbuf(f"xt{i}", [128, D], F32) for i in range(2)]
    sqs = sbuf("sqs", [128, D], F32)
    lnt_t = [sbuf(f"lnt{i}", [128, D], BF16) for i in range(2)]
    FEN0 = seed[0]["s_fence"]
    DSEM0 = [(s, seed[0].get(s, 0)) for s in ["s_x", "s_x2", "s_w", "s_qld", "s_q2", "s_q3", "s_q4", "s_ast", "s_st2", "s_a2", "s_bld", "s_out", "s_cc", "g_ld", "g_st", "s_t1", "s_t2"]]

    def emit_a1a(w):
        T, V, S_, Y, G = w.T, w.V, w.S, w.Y, w.G
        w.block_start(FEN0, DSEM0)
        for t in range(NT):
            bi = t % 2
            sx = "s_x" if bi == 0 else "s_x2"
            sx2 = "s_q3" if bi == 0 else "s_q4"
            if t >= 2:
                Y.wait("s_ln", t - 1)
            w.dma(Y, sx, out=xt[bi][:, 0:1024], in_=dr["x_full"][t * 128:(t + 1) * 128, 0:1024])
            w.dma(Y, sx2, out=xt[bi][:, 1024:2048], in_=dr["x_full"][t * 128:(t + 1) * 128, 1024:2048])
            lv = [(sx, w.c(sx)), (sx2, w.c(sx2))]
            base = 8 * bi
            for s_, v_ in lv:
                V.wait(s_, v_)
            if t >= 2:
                V.wait("s_ln", t - 1)
            V.reduce_sum(colsc[:, base:base + 1], xt[bi][:, :], axis=AX.X)
            for s_, v_ in lv:
                S_.wait(s_, v_)
            S_.wait("s_aux", w.c("s_aux"))
            S_.activation(sqs[:, :], xt[bi][:, :], AF.Square,
                          accum_out=colsc[:, base + 1:base + 2]).inc("s_aux")
            ln_scalars(w, base, D, EPS_P, pre_wait=("s_aux", w.c("s_aux")))
            if t >= 2:
                V.wait("s_tp", 16 * (t - 1))
            V.tensor_scalar(lnt_t[bi][:, :], xt[bi][:, :],
                            colsc[:, base + 2:base + 3],
                            colsc[:, base + 3:base + 4], OP.mult, OP.add).inc("s_ln")
            lnv = w.c("s_ln")
            for qd in range(4):
                slot = qd % 2
                T.wait("s_ln", lnv)
                if t * 4 + qd >= 2:
                    T.wait("s_ev", w.c("s_ev") - 1)
                for j in range(4):
                    c = qd * 4 + j
                    T.transpose(
                        ps[:, slot * 512 + j * 64: slot * 512 + (j + 1) * 64].bitcast(BF16),
                        lnt_t[bi][:, c * 128:(c + 1) * 128], identb[:, :]).inc("s_tp")
                V.wait("s_tp", w.c("s_tp"))
                for j in range(4):
                    c = qd * 4 + j
                    ii = V.tensor_copy(lnT[:, c * SS + t * 128: c * SS + (t + 1) * 128],
                                       ps[:, slot * 512 + j * 64: slot * 512 + (j + 1) * 64].bitcast(BF16))
                    if j == 3:
                        ii.inc("s_ev")
        w.fence()

    run_block(emit_a1a)
    close_last(5)

    wsb = sbuf("wsb", [128, NCHK * 1024], BF16)
    stg = [sbuf(f"stg{i}", [128, 512], BF16) for i in range(2)]
    FEN1 = seed[0]["s_fence"]
    DSEM1 = [(s, seed[0].get(s, 0)) for s in ["s_x", "s_x2", "s_w", "s_qld", "s_q2", "s_q3", "s_q4", "s_ast", "s_st2", "s_a2", "s_bld", "s_out", "s_cc", "g_ld", "g_st", "s_t1", "s_t2"]]
    EV1 = seed[0]["s_ev"]

    def emit_a1b(w):
        T, V, S_, Y, G = w.T, w.V, w.S, w.Y, w.G
        w.block_start(FEN1, DSEM1)
        nmm = [0]

        def mm_unit(stats, movs, n_out, scr_ap, wv):
            slot = nmm[0] % 2
            ssem = "s_ast" if slot == 0 else "s_st2"
            T.wait("s_w", wv)
            if nmm[0] == 0:
                T.wait("s_ev", EV1)
            if nmm[0] >= 2:
                T.wait("s_mm", w.c("s_mm") - 1)
            for ci in range(NCHK):
                ii = T.matmul(ps[:, 2048 + slot * 512: 2048 + slot * 512 + n_out],
                              stats[ci], movs[ci], start=(ci == 0), stop=(ci == NCHK - 1))
                if ci == NCHK - 1:
                    ii.inc("s_sc")
            V.wait("s_sc", w.c("s_sc"))
            if nmm[0] >= 2:
                V.wait(ssem, w.c(ssem))
            V.tensor_copy(stg[slot][:, 0:n_out],
                          ps[:, 2048 + slot * 512: 2048 + slot * 512 + n_out]).inc("s_mm")
            Y.wait("s_mm", w.c("s_mm"))
            w.dma(Y, ssem, out=scr_ap, in_=stg[slot][:, 0:n_out])
            nmm[0] += 1

        for wname, scr in (("wq", q_scr), ("wk", k_scr)):
            G.wait("s_mm", w.c("s_mm"))
            w.dma(G, "s_w", out=wsb[:, :].rearrange("p (c n) -> p c n", c=NCHK),
                  in_=dr[wname].rearrange("(c p) n -> p c n", p=128))
            wv = w.c("s_w")
            for h in range(NHL):
                for br in range(2):
                    off = h * 256 + br * 128
                    for tg in range(SS // 512):
                        mm_unit(
                            [wsb[:, c * 1024 + off: c * 1024 + off + 128] for c in range(NCHK)],
                            [lnT[:, c * SS + tg * 512: c * SS + tg * 512 + 512] for c in range(NCHK)],
                            512, scr.ap()[h, br, :, tg * 512:(tg + 1) * 512], wv)
        G.wait("s_mm", w.c("s_mm"))
        w.dma(G, "s_w", out=wsb[:, :].rearrange("p (c n) -> p c n", c=NCHK),
              in_=dr["wv"].rearrange("(c p) n -> p c n", p=128))
        wv = w.c("s_w")
        for h in range(NHL):
            for t in range(NT):
                mm_unit(
                    [lnT[:, c * SS + t * 128: c * SS + (t + 1) * 128] for c in range(NCHK)],
                    [wsb[:, c * 1024 + h * 256: c * 1024 + h * 256 + VD] for c in range(NCHK)],
                    VD, v_scr.ap()[h, t, :, :], wv)
        w.fence()

    run_block(emit_a1b)
    close_last(4)
    AST_QKV = [("s_ast", seed[0].get("s_ast", 0)), ("s_st2", seed[0].get("s_st2", 0))]

    ksb = sbuf("ksb", [128, 2 * SS], BF16)
    qsb = sbuf("qsb", [128, 2 * SS], BF16)
    vstrip = sbuf("vstrip", [128, NT * VSTR], BF16)
    pts = [sbuf(f"pts{br}", [128, NT * 512], BF16) for br in range(2)]
    osb = sbuf("osb", [128, 3 * VSTR], F32)
    astg = [sbuf(f"astg{i}", [128, 256], BF16) for i in range(2)]
    thw = [sbuf(f"thw{i}", [128, 2048], BF16) for i in range(2)]
    throw = sbuf("throw", [4, D], F32)
    trow = sbuf("trow", [4, D], F32)
    rsq = sbuf("rsq", [4, D], F32)
    trow_b = sbuf("trow_b", [4, D], BF16)
    tT = sbuf("tT", [128, 64], BF16)
    selb = sbuf("selb", [4, 1], F32)
    r4 = sbuf("r4", [4, 16], F32)
    g4a = sbuf("g4a", [4, D], F32)
    g4b = sbuf("g4b", [4, D], F32)
    d2bt = sbuf("d2bt", [4, D], F32)
    zrow = sbuf("zrow", [1, D], F32)
    xlr = sbuf("xlr", [1, D], F32)
    brow = sbuf("brow", [1, 2048], BF16)
    a4c = sbuf("a4c", [128, 16], BF16)
    a4d = sbuf("a4d", [128, 16], BF16)
    FEN2 = seed[0]["s_fence"]
    DSEM2 = [(s, seed[0].get(s, 0)) for s in ["s_x", "s_x2", "s_w", "s_qld", "s_q2", "s_q3", "s_q4", "s_ast", "s_st2", "s_a2", "s_bld", "s_out", "s_cc", "g_ld", "g_st", "s_t1", "s_t2"]]

    def emit_a2(w):
        T, V, S_, Y, G = w.T, w.V, w.S, w.Y, w.G
        w.block_start(FEN2, DSEM2)

        def barrier():
            w.fence()
            fv = w.c("s_fence")
            w.all_wait("s_fence", fv)
            for s in ("s_x", "s_x2", "s_w", "s_qld", "s_q2", "s_q3", "s_q4", "s_ast",
                      "s_st2", "s_a2", "s_bld", "s_out", "s_cc", "g_ld", "g_st",
                      "s_t1", "s_t2"):
                w.all_wait(s, w.c(s))
        if DBG:
            w.dma(Y, "s_q4", out=dbg_id.ap()[2], in_=identb[:, :])
        mem = {"ast_astg": 0, "rowuse": 0}

        ROT = ("s_qld", "s_q2", "s_q3", "s_q4")

        def head_loads(h):
            Y.wait("s_ex", w.c("s_ex"))
            for s, v in AST_QKV:
                Y.wait(s, v)
            i = 0
            for br in range(2):
                w.dma(Y, ROT[i % 2], out=ksb[:, br * SS:(br + 1) * SS],
                      in_=k_scr.ap()[h, br, :, :]); i += 1
                w.dma(Y, ROT[i % 2], out=qsb[:, br * SS:(br + 1) * SS],
                      in_=q_scr.ap()[h, br, :, :]); i += 1
            kqv = [(s, w.c(s)) for s in ROT[0:2]]
            Y.wait("s_av", w.c("s_av"))
            for t in range(NT):
                w.dma(Y, ROT[2 + i % 2], out=vstrip[:, t * VSTR: t * VSTR + VD],
                      in_=v_scr.ap()[h, t, :, :]); i += 1
            return (kqv, [(s, w.c(s)) for s in ROT[2:4]])

        def qgroup(h, qg, ldv, last31=False):
            nkc = 4 * qg + 4
            nquad = (nkc + 3) // 4
            avv_prev = w.c("s_av")
            npair = (nkc + 1) // 2
            for br in range(2):
                for pair in range(npair):
                    slot = pair % 2
                    kcs = [pair * 2 + i for i in range(2) if pair * 2 + i < nkc]
                    if pair == 0:
                        for s, v in ldv[0]:
                            T.wait(s, v)
                    T.wait("s_ex", w.c("s_ex") - 1 if pair >= 2 else w.c("s_ex"))
                    for i, kc in enumerate(kcs):
                        ii = T.matmul(ps[:, slot * 1024 + i * 512: slot * 1024 + i * 512 + 512],
                                      ksb[:, br * SS + kc * 128: br * SS + (kc + 1) * 128],
                                      qsb[:, br * SS + qg * 512: br * SS + qg * 512 + 512],
                                      start=True, stop=True)
                        if i == len(kcs) - 1:
                            ii.inc("s_sc")
                    S_.wait("s_sc", w.c("s_sc"))
                    S_.wait("s_av", avv_prev)
                    wid = len(kcs) * 512
                    S_.activation(pts[br][:, kcs[0] * 512: kcs[0] * 512 + wid],
                                  ps[:, slot * 1024: slot * 1024 + wid],
                                  AF.Exp, scale=SCALE).inc("s_ex")
                exv = w.c("s_ex")
                first = True
                last = None
                for t in range(4):
                    qt = 4 * qg + t
                    if first:
                        V.wait("s_ex", exv)
                        V.wait("s_av", avv_prev)
                        first = False
                    last = V.tensor_tensor(
                        pts[br][:, qt * 512 + t * 128: qt * 512 + (t + 1) * 128],
                        pts[br][:, qt * 512 + t * 128: qt * 512 + (t + 1) * 128],
                        tri[:, :], OP.mult)
                    for kc in range(qt + 1, nkc):
                        last = V.memset(pts[br][:, kc * 512 + t * 128: kc * 512 + (t + 1) * 128], 0.0)
                last.inc("s_mk")
            mkv = w.c("s_mk")
            for t in range(4):
                qt = 4 * qg + t
                if t == 0:
                    for s, v in ldv[1]:
                        T.wait(s, v)
                for br in range(2):
                    obase = 2048 + br * 512
                    T.wait("s_mk", mkv)
                    T.wait("s_cmb", w.c("s_cmb"))
                    for kc in range(qt + 1):
                        ii = T.matmul(ps[:, obase: obase + VSTR],
                                      pts[br][:, kc * 512 + t * 128: kc * 512 + (t + 1) * 128],
                                      vstrip[:, kc * VSTR: kc * VSTR + VSTR],
                                      start=(kc == 0), stop=(kc == qt))
                        if kc == qt:
                            ii.inc("s_av")
                V.wait("s_av", w.c("s_av"))
                V.wait("s_aux", w.c("s_aux"))
                V.tensor_copy(osb[:, 0:VSTR], ps[:, 2048: 2048 + VSTR])
                V.tensor_copy(osb[:, VSTR: 2 * VSTR], ps[:, 2560: 2560 + VSTR])
                V.drain()
                V.reciprocal(colsc[:, 16:17], osb[:, VD:VD + 1])
                V.reciprocal(colsc[:, 17:18], osb[:, VSTR + VD:VSTR + VD + 1])
                V.drain()
                V.tensor_scalar(colsc[:, 18:19], colsc[:, 17:18], neglam[:, :], None, OP.mult)
                V.tensor_scalar(osb[:, 0:VD], osb[:, 0:VD], colsc[:, 16:17], None, OP.mult)
                V.drain()
                V.scalar_tensor_tensor(osb[:, VSTR:VSTR + VD], osb[:, VSTR:VSTR + VD],
                                       colsc[:, 18:19], osb[:, 0:VD], OP.mult, OP.add).inc("s_cmb")
                S_.wait("s_cmb", w.c("s_cmb"))
                S_.activation(osb[:, 2 * VSTR: 2 * VSTR + VD], osb[:, VSTR:VSTR + VD], AF.Square,
                              accum_out=colsc[:, 19:20]).inc("s_aux")
                V.wait("s_aux", w.c("s_aux"))
                V.tensor_scalar(colsc[:, 20:21], colsc[:, 19:20], 1.0 / VD, float(EPS_P),
                                OP.mult, OP.add).inc("s_stat")
                S_.wait("s_stat", w.c("s_stat"))
                S_.activation(colsc[:, 21:22], colsc[:, 20:21], AF.Sqrt).inc("s_aux")
                V.wait("s_aux", w.c("s_aux"))
                V.reciprocal(colsc[:, 22:23], colsc[:, 21:22])
                V.drain()
                V.wait("s_ast", w.c("s_ast"))
                V.wait("s_a2", w.c("s_a2"))
                V.wait("g_st", w.c("g_st"))
                V.scalar_tensor_tensor(astg[0][:, :], osb[:, VSTR:VSTR + VD],
                                       colsc[:, 22:23], subl_at[:, :], OP.mult, OP.mult)
                ii = V.scalar_tensor_tensor(astg[1][:, :], osb[:, VSTR:VSTR + VD],
                                            colsc[:, 22:23], subl_bt[:, :], OP.mult, OP.mult)
                ii.inc("s_cmb")
                shard = 0 if qt < NTH else 1
                row = (qt % NTH) * 128
                Y.wait("s_cmb", w.c("s_cmb"))
                w.dma(Y, "s_ast", out=contrib[h].ap()[shard, row:row + 128, 0:256],
                      in_=astg[0][:, :])
                w.dma(Y, "s_a2", out=contrib[h].ap()[shard, row:row + 128, 256:512],
                      in_=astg[1][:, :])
                if last31 and qt == NT - 1:
                    w.dma(Y, "s_ast", out=a31_i.ap()[0:1, h * 256:(h + 1) * 256],
                          in_=astg[0][127:128, :])
                    w.dma(Y, "s_a2", out=a31_j.ap()[0:1, h * 256:(h + 1) * 256],
                          in_=astg[1][127:128, :])

        def row_matmul(M, nchk_, ngrp, wrows_fn, statT_fn, out_buf, bias_slice=None,
                       act=AF.Copy):
            for rnd in range((ngrp + 1) // 2):
                grps = [g for g in (2 * rnd, 2 * rnd + 1) if g < ngrp]
                for c in range(nchk_):
                    csem = "s_t1" if c % 2 == 0 else "s_t2"
                    mem["rowuse"] += 1
                    if mem["rowuse"] >= 3:
                        Y.wait("s_row", mem["rowuse"] - 2)
                    w.dma(Y, csem, out=thw[c % 2][:, 0:ngrp * 512], in_=wrows_fn(c))
                    T.wait(csem, w.c(csem))
                    if c == 0:
                        T.wait("s_ev", w.c("s_ev"))
                    lastmm = None
                    for gi, gg in enumerate(grps):
                        lastmm = T.matmul(ps[0:M, 3072 + gi * 512: 3072 + gi * 512 + 512],
                                          statT_fn(c), thw[c % 2][:, gg * 512:(gg + 1) * 512],
                                          start=(c == 0), stop=(c == nchk_ - 1 and bias_slice is None))
                        if c == nchk_ - 1 and bias_slice is not None:
                            lastmm = T.matmul(ps[0:M, 3072 + gi * 512: 3072 + gi * 512 + 512],
                                              ones1b[0:1, 0:M], bias_slice(gg),
                                              start=False, stop=True)
                    lastmm.inc("s_row")
                S_.wait("s_row", mem["rowuse"])
                for gi, gg in enumerate(grps):
                    ii = S_.activation(out_buf[0:M, gg * 512:(gg + 1) * 512],
                                       ps[0:M, 3072 + gi * 512: 3072 + gi * 512 + 512], act)
                    if gi == len(grps) - 1:
                        ii.inc("s_ev")

        def row_transpose(src_f32, nchk_):
            T.wait("s_ev", w.c("s_ev"))
            for c in range(nchk_):
                T.transpose(ps[:, 3072 + c * 4: 3072 + c * 4 + 4],
                            src_f32[0:4, c * 128:(c + 1) * 128], identf[0:4, 0:4]).inc("s_sc")
            V.wait("s_sc", w.c("s_sc"))
            V.tensor_copy(tT[:, 0:nchk_ * 4], ps[:, 3072: 3072 + nchk_ * 4]).inc("s_ev")
            return w.c("s_ev")

        def row_ln(P, buf, n_el, eps, sb_, gamma=None, beta=None):
            cs = lambda i: r4[0:P, sb_ + i: sb_ + i + 1]
            V.reduce_sum(cs(0), buf[0:P, 0:n_el], axis=AX.X).inc("s_stat")
            S_.wait("s_stat", w.c("s_stat"))
            S_.wait("s_aux", w.c("s_aux"))
            S_.activation(rsq[0:P, 0:n_el], buf[0:P, 0:n_el], AF.Square,
                          accum_out=cs(1)).inc("s_aux")
            V.wait("s_aux", w.c("s_aux"))
            V.tensor_scalar(cs(0), cs(0), 1.0 / n_el, None, OP.mult)
            V.tensor_scalar(cs(1), cs(1), 1.0 / n_el, None, OP.mult)
            V.drain()
            V.tensor_scalar(cs(2), cs(0), cs(0), None, OP.mult)
            V.drain()
            V.tensor_tensor(cs(1), cs(1), cs(2), OP.subtract)
            V.drain()
            V.tensor_scalar(cs(1), cs(1), float(eps), None, OP.add).inc("s_stat")
            S_.wait("s_stat", w.c("s_stat"))
            S_.wait("s_aux", w.c("s_aux"))
            S_.activation(cs(1), cs(1), AF.Sqrt).inc("s_aux")
            V.wait("s_aux", w.c("s_aux"))
            V.reciprocal(cs(2), cs(1))
            V.drain()
            V.tensor_scalar(cs(3), cs(0), cs(2), -1.0, OP.mult, OP.mult)
            V.drain()
            V.tensor_scalar(buf[0:P, 0:n_el], buf[0:P, 0:n_el], cs(2), cs(3),
                            OP.mult, OP.add)
            if gamma is not None:
                V.drain()
                V.tensor_tensor(buf[0:P, 0:n_el], buf[0:P, 0:n_el], gamma[0:P, 0:n_el], OP.mult)
                V.drain()
                V.tensor_tensor(buf[0:P, 0:n_el], buf[0:P, 0:n_el], beta[0:P, 0:n_el], OP.add)
            V.drain()

        V.wait("ini", 1)
        V.memset(vstrip[:, VD::VSTR], 1.0).inc("s_mk")
        for h in range(NHL):
            ldv = head_loads(h)
            qgroup(h, NQG - 1, ldv, last31=True)
        if DBG:
            w.dma(Y, "s_q4", out=dbg_id.ap()[3], in_=identb[:, :])
        G.wait("s_ast", w.c("s_ast"))
        G.wait("s_a2", w.c("s_a2"))
        G.collective_compute("AllGather", OP.bypass, replica_groups=PAIRS,
                             ins=[a31_i.ap().opt()], outs=[a31_o.ap().opt()]).inc("s_cc")
        G.collective_compute("AllGather", OP.bypass, replica_groups=PAIRS,
                             ins=[a31_j.ap().opt()], outs=[a31_p.ap().opt()]).inc("s_cc")
        ccv = w.c("s_cc")
        ldv0 = head_loads(0)
        qgroup(0, 0, ldv0)
        if NQG - 1 > 1:
            qgroup(0, 1, ldv0)
        for ci, blk in enumerate(WO_PERM):
            Y.wait("s_cc", ccv)
            w.dma(Y, "s_qld" if ci % 2 == 0 else "s_q2", out=a4c[:, ci:ci + 1],
                  in_=a31_o.ap()[blk // 8: blk // 8 + 1,
                                 (blk % 8) * 128:(blk % 8) * 128 + 128],
                  allow_slow_non_contiguous=True)
            w.dma(Y, "s_q2" if ci % 2 == 0 else "s_qld", out=a4d[:, ci:ci + 1],
                  in_=a31_p.ap()[blk // 8: blk // 8 + 1,
                                 (blk % 8) * 128:(blk % 8) * 128 + 128],
                  allow_slow_non_contiguous=True)
        if DBG:
            w.dma(Y, "s_q4", out=dbg_id.ap()[7], in_=identb[:, :])
        V.wait("s_qld", w.c("s_qld"))
        V.wait("s_q2", w.c("s_q2"))
        V.tensor_tensor(a4c[:, :], a4c[:, :], a4d[:, :], OP.add).inc("s_stat")
        T.wait("s_stat", w.c("s_stat"))
        T.wait("s_qld", w.c("s_qld"))
        T.wait("s_q2", w.c("s_q2"))
        T.wait("s_w", WPRE)
        Y.wait("s_w", WPRE)
        row_matmul(1, NCHK, 4, lambda c: wo_bf.ap()[c * 128:(c + 1) * 128, :],
                   lambda c: a4c[:, c:c + 1], zrow)
        if DBG:
            Y.wait("s_ev", w.c("s_ev"))
            w.dma(Y, "s_q4", out=dbg_id.ap()[8], in_=identb[:, :])
        V.wait("s_ev", w.c("s_ev"))
        row_ln(1, zrow, D, EPS_P, 0)
        w.dma(Y, "s_x", out=xlr[0:1, :], in_=dr["x_last"])
        V.wait("s_x", w.c("s_x"))
        V.tensor_tensor(zrow[0:1, :], zrow[0:1, :], xlr[0:1, :], OP.add)
        V.drain()
        row_ln(1, zrow, D, EPS_P, 4)
        V.tensor_copy(zrow[0:1, :], zrow[0:1, :]).inc("s_stat")
        Y.wait("s_stat", w.c("s_stat"))
        w.dma(Y, "s_ast", out=z4_i.ap()[0:1, :], in_=zrow[0:1, :])
        if DBG:
            w.dma(Y, "s_q4", out=dbg_id.ap()[9], in_=identb[:, :])
        G.wait("s_ast", w.c("s_ast"))
        G.collective_compute("AllGather", OP.bypass, replica_groups=ALL8,
                             ins=[z4_i.ap().opt()], outs=[z4_o.ap().opt()]).inc("s_cc")
        ccv = w.c("s_cc")
        Y.wait("s_cc", ccv)
        w.dma(Y, "s_qld", out=throw[0:4, :], in_=z4_o.ap()[0:8:2, :])
        w.dma(Y, "s_qld", out=g4a[:, :], in_=dr["in_gamma"].broadcast_to([4, D]))
        w.dma(Y, "s_qld", out=g4b[:, :], in_=dr["in_beta"].broadcast_to([4, D]))
        w.dma(Y, "s_qld", out=d2bt[:, :], in_=dr["d2_b"].broadcast_to([4, D]))
        w.dma(Y, "s_qld", out=selb[:, :], in_=dr["sel_b"])
        V.wait("s_qld", w.c("s_qld"))
        row_ln(4, throw, D, EPS_A, 8, gamma=g4a, beta=g4b)
        V.memset(r4[0:4, 15:16], 0.0).inc("s_stat")
        if DBG:
            Y.wait("s_stat", w.c("s_stat"))
            w.dma(Y, "s_q4", out=dbg_id.ap()[10], in_=identb[:, :])
        T.wait("s_stat", w.c("s_stat"))
        row_transpose(throw, NCHK)
        w.dma(Y, "s_qld", out=brow[0:1, 0:D], in_=rb_bf.ap()[0:1, :])
        if DBG:
            Y.wait("s_ev", w.c("s_ev"))
            w.dma(Y, "s_q4", out=dbg_id.ap()[11], in_=identb[:, :])
        T.wait("s_qld", w.c("s_qld"))
        row_matmul(4, NCHK, 4, lambda c: agg_bf.ap()[c * 128:(c + 1) * 128, :],
                   lambda c: tT[:, c * 4:(c + 1) * 4], throw,
                   bias_slice=lambda gg: brow[0:1, gg * 512:(gg + 1) * 512])
        if DBG:
            w.dma(Y, "s_q4", out=dbg_id.ap()[4], in_=identb[:, :])
        Y.wait("s_ev", w.c("s_ev"))
        w.dma(Y, "s_qld", out=g4a[:, :], in_=dr["pn_gamma"].broadcast_to([4, D]))
        w.dma(Y, "s_qld", out=g4b[:, :], in_=dr["pn_beta"].broadcast_to([4, D]))
        pnv = w.c("s_qld")
        barrier()

        def thought_pre(k):
            V.wait("s_ev", w.c("s_ev"))
            V.wait("s_aux", w.c("s_aux"))
            V.wait("s_qld", pnv)
            V.tensor_copy(trow[0:4, :], throw[0:4, :])
            V.drain()
            row_ln(4, trow, D, EPS_A, 8, gamma=g4a, beta=g4b)
            V.memset(r4[0:4, 15:16], 0.0).inc("s_stat")
            T.wait("s_stat", w.c("s_stat"))
            row_transpose(trow, NCHK)
            Y.wait("s_row", mem["rowuse"])
            w.dma(Y, "s_qld", out=brow[0:1, 0:HFS], in_=rb_bf.ap()[3:4, 0:HFS])
            T.wait("s_qld", w.c("s_qld"))
            row_matmul(4, NCHK, 2, lambda c: d1_bf.ap()[c * 128:(c + 1) * 128, :],
                       lambda c: tT[:, c * 4:(c + 1) * 4], trow,
                       bias_slice=lambda gg: brow[0:1, gg * 512:(gg + 1) * 512], act=AF.Gelu)
            row_transpose(trow, HFS // 128)
            row_matmul(4, HFS // 128, 4, lambda c: d2_bf.ap()[c * 128:(c + 1) * 128, :],
                       lambda c: tT[:, c * 4:(c + 1) * 4], trow)
            V.wait("s_ev", w.c("s_ev"))
            V.tensor_copy(trow[0:4, :], trow[0:4, :]).inc("s_stat")
            Y.wait("s_stat", w.c("s_stat"))
            w.dma(Y, "s_ast", out=ar_i[k].ap()[0:4, :], in_=trow[0:4, :])
            G.wait("s_ast", w.c("s_ast"))
            G.collective_compute("AllReduce", OP.add, replica_groups=ALL8,
                                 ins=[ar_i[k].ap().opt()], outs=[ar_o[k].ap().opt()]).inc("s_cc")
            mem[f"ar{k}"] = w.c("s_cc")

        def thought_post(k):
            Y.wait("s_cc", mem[f"ar{k}"])
            w.dma(Y, "s_qld", out=trow[0:4, :], in_=ar_o[k].ap()[0:4, :])
            V.wait("s_qld", w.c("s_qld"))
            V.tensor_tensor(trow[0:4, :], trow[0:4, :], d2bt[0:4, :], OP.add)
            V.drain()
            V.tensor_tensor(throw[0:4, :], throw[0:4, :], trow[0:4, :], OP.add).inc("s_aux")

        def bc_final():
            V.wait("s_aux", w.c("s_aux"))
            T.wait("s_aux", w.c("s_aux"))
            row_transpose(throw, NCHK)
            Y.wait("s_row", mem["rowuse"])
            w.dma(Y, "s_qld", out=brow[0:1, 0:D], in_=rb_bf.ap()[1:2, :])
            T.wait("s_qld", w.c("s_qld"))
            row_matmul(4, NCHK, 4, lambda c: bcw_bf.ap()[c * 128:(c + 1) * 128, :],
                       lambda c: tT[:, c * 4:(c + 1) * 4], trow,
                       bias_slice=lambda gg: brow[0:1, gg * 512:(gg + 1) * 512])
            V.wait("s_ev", w.c("s_ev"))
            T.wait("s_ev", w.c("s_ev"))
            for og in range(4):
                pcol = 3072 + (og % 2) * 512
                if og >= 2:
                    T.wait("s_ev", w.c("s_ev") - 1)
                ii = T.matmul(ps[0:1, pcol: pcol + 512], selb[0:4, 0:1],
                              trow[0:4, og * 512:(og + 1) * 512], start=True, stop=True)
                ii.inc("s_sc")
                V.wait("s_sc", w.c("s_sc"))
                V.tensor_copy(zrow[0:1, og * 512:(og + 1) * 512],
                              ps[0:1, pcol: pcol + 512]).inc("s_ev")
            V.drain()
            V.tensor_copy(zrow[0:1, :], zrow[0:1, :]).inc("s_stat")
            Y.wait("s_stat", w.c("s_stat"))
            w.dma(Y, "s_ast", out=bc_scr.ap()[0:1, 0:D], in_=zrow[0:1, :])
            w.led["bc_ast"] = w.c("s_ast")

        for h in range(NHL):
            if h == 0:
                ldv = ldv0
                thought_pre(0)
                qstart = min(2, NQG - 1)
            else:
                ldv = head_loads(h)
                qgroup(h, 0, ldv)
                thought_post(h - 1)
                thought_pre(h)
                qstart = 1
            for qg in range(qstart, NQG - 1):
                qgroup(h, qg, ldv)
            G.wait("s_ast", w.c("s_ast"))
            G.wait("s_a2", w.c("s_a2"))
            G.collective_compute("ReduceScatter", OP.add, replica_groups=PAIRS,
                                 ins=[contrib[h].ap().opt()],
                                 outs=[rs_out[h].ap().opt()]).inc("s_cc")
        thought_post(NHL - 1)
        barrier()
        bc_final()
        if DBG:
            w.dma(Y, "s_q4", out=dbg_id.ap()[6], in_=identb[:, :])
        w.fence()

    run_block(emit_a2)
    RS_DONE = seed[0]["s_cc"]
    TH_DONE = seed[0]["bc_ast"]
    close_last(25)

    wo_sb = sbuf("wo_sb", [128, NCHK * D], BF16)
    gw_sb = sbuf("gw_sb", [128, NCHK * D], BF16)
    a_sb = sbuf("a_sb", [128, D], BF16)
    aT = sbuf("aT", [128, D], BF16)
    zT = sbuf("zT", [128, D], BF16)
    gb_bf = sbuf("gb_bf", [1, D], BF16)
    bcB = sbuf("bcB", [128, D], F32)
    outgT = sbuf("outgT", [128, D], F32)
    outbT = sbuf("outbT", [128, D], F32)
    bA = sbuf("bA", [128, D], F32)
    bX = sbuf("bX", [128, D], F32)
    bC = sbuf("bC", [128, D], F32)
    bD = sbuf("bD", [128, D], F32)
    bE = sbuf("bE", [128, D], F32)
    FEN3 = seed[0]["s_fence"]
    DSEM3 = [(s, seed[0].get(s, 0)) for s in ["s_x", "s_x2", "s_w", "s_qld", "s_q2", "s_q3", "s_q4", "s_ast", "s_st2", "s_a2", "s_bld", "s_out", "s_cc", "g_ld", "g_st", "s_t1", "s_t2"]]

    def emit_b(w):
        T, V, S_, Y, G = w.T, w.V, w.S, w.Y, w.G
        w.block_start(FEN3, DSEM3)
        mem = {}
        Y.wait("s_w", WPRE)
        HC = NCHK // 2
        w.dma(Y, "s_bld", out=wo_sb[:, 0:HC * D].rearrange("p (c n) -> p c n", c=HC),
              in_=wo_bf.ap()[0:HC * 128, :].rearrange("(c p) n -> p c n", p=128))
        w.dma(Y, "s_st2", out=wo_sb[:, HC * D:].rearrange("p (c n) -> p c n", c=HC),
              in_=wo_bf.ap()[HC * 128:, :].rearrange("(c p) n -> p c n", p=128))
        w.dma(Y, "s_qld", out=gw_sb[:, 0:HC * D].rearrange("p (c n) -> p c n", c=HC),
              in_=gw_bf.ap()[0:HC * 128, :].rearrange("(c p) n -> p c n", p=128))
        w.dma(Y, "s_a2", out=gw_sb[:, HC * D:].rearrange("p (c n) -> p c n", c=HC),
              in_=gw_bf.ap()[HC * 128:, :].rearrange("(c p) n -> p c n", p=128))
        w.dma(Y, "s_q2", out=gb_bf[0:1, :], in_=rb_bf.ap()[2:3, :])
        Y.wait("s_ast", TH_DONE)
        w.dma(Y, "s_q2", out=bcB[:, :], in_=bc_scr.ap()[0:1, 0:D].broadcast_to([128, D]))
        w.dma(Y, "s_q2", out=outgT[:, :], in_=dr["out_gamma"].broadcast_to([128, D]))
        w.dma(Y, "s_q2", out=outbT[:, :], in_=dr["out_beta"].broadcast_to([128, D]))
        BLD0 = [("s_bld", w.c("s_bld")), ("s_st2", w.c("s_st2")),
                ("s_qld", w.c("s_qld")), ("s_a2", w.c("s_a2")), ("s_q2", w.c("s_q2"))]
        GB0 = ("s_q2", w.c("s_q2"))

        for tt in range(NTH):
            Y.wait("s_cc", RS_DONE)
            if tt > 0:
                Y.wait("s_tp", mem["a_tp"])
            for h in range(NHL):
                w.dma(Y, "s_q3" if h % 2 == 0 else "s_q4",
                      out=a_sb[:, h * 512:(h + 1) * 512],
                      in_=rs_out[h].ap()[tt * 128:(tt + 1) * 128, :])
            ldv = [("s_q3", w.c("s_q3")), ("s_q4", w.c("s_q4"))]
            if tt > 0:
                Y.wait("s_stat", mem["x1_done"])
            for xi, sxn in enumerate(["s_x", "s_x2", "s_t1", "s_t2"]):
                w.dma(Y, sxn, out=bX[:, xi * 512:(xi + 1) * 512],
                      in_=dr["x_half"][tt * 128:(tt + 1) * 128, xi * 512:(xi + 1) * 512])
            xldv = [(s_, w.c(s_)) for s_ in ["s_x", "s_x2", "s_t1", "s_t2"]]
            if tt > 0:
                Y.wait("s_bev", mem["bev"])
                w.dma(Y, "s_out", out=out_d[(tt - 1) * 128:tt * 128, :], in_=bD[:, :])
            if DBG and tt == 0:
                for s, v in ldv:
                    Y.wait(s, v)
                w.dma(Y, "s_q2", out=dbg_t.ap()[2], in_=a_sb[:, :])
                w.dma(Y, "s_q4", out=dbg_id.ap()[0], in_=identb[:, :])
                w.dma(Y, "s_q4", out=dbg_id.ap()[1], in_=identb[:, :])
            for s, v in BLD0 + ldv:
                T.wait(s, v)
            if tt > 0:
                T.wait("s_ev", mem["ztv"])
            for qd in range(4):
                for j in range(4):
                    c = qd * 4 + j
                    T.transpose(
                        ps[:, 2048 + qd * 512 + j * 64: 2048 + qd * 512 + (j + 1) * 64].bitcast(BF16),
                        a_sb[:, c * 128:(c + 1) * 128], identb[:, :]).inc("s_tp")
            mem["a_tp"] = w.c("s_tp")
            if tt == 0:
                S_.wait("s_tp", w.c("s_tp"))
                for qd in range(4):
                    ii = S_.activation(aT[:, qd * 512:(qd + 1) * 512],
                                       ps[:, 2048 + qd * 512: 2048 + qd * 512 + 256].bitcast(BF16),
                                       AF.Copy)
                    if qd == 3:
                        ii.inc("s_ev")
                atv = w.c("s_ev")
            else:
                atv = mem["atv_next"]
            if DBG and tt == 0:
                Y.wait("s_ev", atv)
                w.dma(Y, "s_q2", out=dbg_t.ap()[0], in_=aT[:, :])
            for og in range(4):
                T.wait("s_ev", atv)
                T.wait("s_bmm", w.c("s_bmm"))
                for c in range(NCHK):
                    ii = T.matmul(ps[:, og * 512: og * 512 + 512],
                                  aT[:, c * 128:(c + 1) * 128],
                                  wo_sb[:, c * D + og * 512: c * D + og * 512 + 512],
                                  start=(c == 0), stop=(c == NCHK - 1))
                    if c == NCHK - 1:
                        ii.inc("s_sc")
                S_.wait("s_sc", w.c("s_sc"))
                if og == 0:
                    S_.wait("s_stat", mem.get("x1_done", 0))
                    S_.wait("s_bev", tt)
                S_.activation(bA[:, og * 512:(og + 1) * 512], ps[:, og * 512: og * 512 + 512],
                              AF.Copy, accum_out=colsc[:, 16 + og:17 + og]).inc("s_bmm")
            mem["wo_sc"] = w.c("s_sc")
            if DBG and tt == 0:
                Y.wait("s_bmm", w.c("s_bmm"))
                w.dma(Y, "s_q2", out=dbg_b.ap()[0], in_=bA[:, :])
                w.dma(Y, "s_q2", out=dbg_b.ap()[5], in_=bX[:, :])
            V.wait("s_bmm", w.c("s_bmm"))
            V.reduce_sum(colsc[:, 0:1], colsc[:, 16:20], axis=AX.X)
            S_.wait("s_bmm", w.c("s_bmm"))
            S_.activation(bE[:, :], bA[:, :], AF.Square,
                          accum_out=colsc[:, 1:2]).inc("s_aux")
            ln_scalars(w, 0, D, EPS_P, pre_wait=("s_aux", w.c("s_aux")))
            for s_, v_ in xldv:
                V.wait(s_, v_)
            if DBG and tt == 0:
                V.wait("s_q2", w.c("s_q2"))
            V.tensor_scalar(bA[:, :], bA[:, :], colsc[:, 2:3], colsc[:, 3:4], OP.mult, OP.add)
            V.drain()
            V.scalar_tensor_tensor(bC[:, :], bA[:, :], 1.0, bX[:, :], OP.mult, OP.add,
                                   accum_out=colsc[:, 8:9]).inc("s_stat")
            mem["x1_done"] = w.c("s_stat")
            if DBG:
                S_.wait("s_q2", w.c("s_q2"))
            if DBG and tt == 0:
                Y.wait("s_stat", w.c("s_stat"))
                w.dma(Y, "s_q2", out=dbg_b.ap()[1], in_=bC[:, :])
                w.dma(Y, "s_q2", out=dbg_b.ap()[6][:, 0:32], in_=colsc[:, 0:32])
            S_.wait("s_stat", w.c("s_stat"))
            S_.activation(bA[:, :], bC[:, :], AF.Square, accum_out=colsc[:, 9:10]).inc("s_aux")
            ln_scalars(w, 8, D, EPS_P, pre_wait=("s_aux", w.c("s_aux")))
            if tt > 0:
                V.wait("s_tp", mem["z_tp"])
                V.wait("s_out", tt * 16)
            V.tensor_scalar(bD[:, :], bC[:, :], colsc[:, 10:11], colsc[:, 11:12],
                            OP.mult, OP.add).inc("s_stat")
            zv = w.c("s_stat")
            if DBG and tt == 0:
                Y.wait("s_stat", zv)
                w.dma(Y, "s_q2", out=dbg_b.ap()[2], in_=bD[:, :])
            T.wait("s_stat", zv)
            T.wait("s_ev", w.c("s_ev"))
            for qd in range(4):
                for j in range(4):
                    c = qd * 4 + j
                    T.transpose(ps[:, 2048 + qd * 512 + j * 128: 2048 + qd * 512 + (j + 1) * 128],
                                bD[:, c * 128:(c + 1) * 128], identf[:, :]).inc("s_tp")
            mem["z_tp"] = w.c("s_tp")
            V.wait("s_tp", w.c("s_tp"))
            V.wait("s_sc", mem.get("gt_sc", 0))
            if DBG:
                V.wait("s_q2", w.c("s_q2"))
            for qd in range(4):
                ii = V.tensor_copy(zT[:, qd * 512:(qd + 1) * 512],
                                   ps[:, 2048 + qd * 512: 2048 + qd * 512 + 512])
                if qd == 3:
                    ii.inc("s_ev")
            ztv = w.c("s_ev")
            mem["ztv"] = ztv
            if DBG and tt == 0:
                Y.wait("s_ev", ztv)
                w.dma(Y, "s_q2", out=dbg_t.ap()[1], in_=zT[:, :])
            for og in range(4):
                T.wait("s_ev", ztv)
                T.wait("s_bmm", w.c("s_bmm"))
                for c in range(NCHK):
                    T.matmul(ps[:, og * 512: og * 512 + 512],
                             zT[:, c * 128:(c + 1) * 128],
                             gw_sb[:, c * D + og * 512: c * D + og * 512 + 512],
                             start=(c == 0), stop=False)
                T.matmul(ps[:, og * 512: og * 512 + 512],
                         ones1b[0:1, 0:128], gb_bf[0:1, og * 512:(og + 1) * 512],
                         start=False, stop=True).inc("s_sc")
                S_.wait("s_sc", w.c("s_sc"))
                S_.activation(bE[:, og * 512:(og + 1) * 512], ps[:, og * 512: og * 512 + 512],
                              AF.Sigmoid).inc("s_bmm")
            mem["gt_sc"] = w.c("s_sc")
            if tt + 1 < NTH:
                S_.wait("s_tp", w.c("s_tp") + 16)
                S_.wait("s_sc", mem["wo_sc"])
                for qd in range(4):
                    ii = S_.activation(aT[:, qd * 512:(qd + 1) * 512],
                                       ps[:, 2048 + qd * 512: 2048 + qd * 512 + 256].bitcast(BF16),
                                       AF.Copy)
                    if qd == 3:
                        ii.inc("s_ev")
                mem["atv_next"] = w.c("s_ev")
            if DBG and tt == 0:
                Y.wait("s_bmm", w.c("s_bmm"))
                w.dma(Y, "s_q2", out=dbg_b.ap()[3], in_=bE[:, :])
                w.dma(Y, "s_q2", out=dbg_b.ap()[7], in_=bcB[:, :])
            V.wait("s_bmm", w.c("s_bmm"))
            V.wait(*GB0)
            if DBG and tt == 0:
                V.wait("s_q2", w.c("s_q2"))
            V.tensor_tensor(bE[:, :], bE[:, :], bcB[:, :], OP.mult)
            V.drain()
            V.scalar_tensor_tensor(bE[:, :], bE[:, :], 1.0, bD[:, :], OP.mult, OP.add,
                                   accum_out=colsc[:, 0:1]).inc("s_stat")
            if DBG and tt == 0:
                Y.wait("s_stat", w.c("s_stat"))
                w.dma(Y, "s_q2", out=dbg_b.ap()[4], in_=bE[:, :])
            S_.wait("s_stat", w.c("s_stat"))
            S_.activation(bA[:, :], bE[:, :], AF.Square, accum_out=colsc[:, 1:2]).inc("s_aux")
            ln_scalars(w, 0, D, EPS_A, pre_wait=("s_aux", w.c("s_aux")))
            if DBG and tt == 0:
                V.wait("s_q2", w.c("s_q2"))
            V.tensor_scalar(bE[:, :], bE[:, :], colsc[:, 2:3], colsc[:, 3:4], OP.mult, OP.add)
            V.drain()
            V.tensor_tensor(bE[:, :], bE[:, :], outgT[:, :], OP.mult)
            V.drain()
            V.scalar_tensor_tensor(bE[:, :], bE[:, :], 1.0, outbT[:, :], OP.mult, OP.add,
                                   accum_out=colsc[:, 8:9]).inc("s_stat")
            S_.wait("s_stat", w.c("s_stat"))
            S_.activation(bA[:, :], bE[:, :], AF.Square, accum_out=colsc[:, 9:10]).inc("s_aux")
            ln_scalars(w, 8, D, EPS_P, pre_wait=("s_aux", w.c("s_aux")))
            V.tensor_scalar(bE[:, :], bE[:, :], colsc[:, 10:11], colsc[:, 11:12],
                            OP.mult, OP.add)
            V.drain()
            V.tensor_tensor(bD[:, :], bE[:, :], bC[:, :], OP.add).inc("s_bev")
            mem["bev"] = w.c("s_bev")
        Y.wait("s_bev", mem["bev"])
        w.dma(Y, "s_out", out=out_d[(NTH - 1) * 128:NTH * 128, :], in_=bD[:, :])
        if DBG:
            Y.wait("s_out", NTH * 16)
            w.dma(Y, "s_q3", out=dbg_ct.ap(), in_=contrib[0].ap())
            w.dma(Y, "s_q3", out=dbg_rs.ap(), in_=rs_out[0].ap())
            w.dma(Y, "s_q3", out=dbg_z4i.ap(), in_=z4_i.ap())
            w.dma(Y, "s_q3", out=dbg_z4o.ap(), in_=z4_o.ap())
            Y.wait("s_q3", w.c("s_q3"))
        Y.wait("s_out", NTH * 16)

    run_block(emit_b)
    n_inst = sum(1 for _ in nc.all_instructions())
    return nc, n_inst


_NC_CACHE = {}


def _get_nc(SS):
    if SS not in _NC_CACHE:
        _NC_CACHE[SS] = build_nc(SS)
    return _NC_CACHE[SS]


def kernel(**inputs):
    SS = int(np.asarray(inputs["L0_attn_x"]).shape[1])
    nc, _ = _get_nc(SS)
    in_maps = _stage(inputs, SS)
    res = run_bass_kernel_spmd(nc, in_maps, core_ids=list(range(8)))
    out = np.empty((B, SS, D), dtype=np.float32)
    for c in range(8):
        b, g = c // 2, c % 2
        out[b, g * (SS // 2):(g + 1) * (SS // 2)] = res.results[c]["out"]
    return out
